# version 04
# baseline (speedup 1.0000x reference)
import numpy as np
from contextlib import ExitStack

import concourse.bass as bass
import concourse.mybir as mybir
from concourse.bass_utils import run_bass_kernel_spmd

F32 = mybir.dt.float32
BF16 = mybir.dt.bfloat16
ALU = mybir.AluOpType
AF = mybir.ActivationFunctionType
AX = mybir.AxisListType

NCORES = 8
B_FULL = 16
BB = B_FULL // NCORES
S = 2048
D = 1024
H = 16
HD = 64
FQKV = 3 * D
NT = S // 128
NTILES = BB * NT
I_FFN = 2816
I_HALF = I_FFN // 2
EPS = 1e-5

MIDS = [0.25, 0.75, 1.25, 1.75, 2.5, 3.5, 5.0]
W2 = [0.5, 0.5, 0.5, 0.5, 1.0, 1.0, 2.0]

ENGINES = ("sync", "vector", "scalar", "tensor", "gpsimd")
DBG_GIDX = 0


def _nop(eng):
    return eng.nop()


class Ctx:

    def __init__(self, nc, est):
        self.nc = nc
        self.est = est
        self.q = {e: [] for e in ENGINES}
        self.sems = {}
        self.val = {}

    def sem(self, name):
        if name not in self.sems:
            self.sems[name] = self.est.enter_context(self.nc.semaphore(name))
            self.val[name] = 0
        return self.sems[name]

    def op(self, engine, fn, inc=None):
        if inc is None:
            self.q[engine].append(fn)
        else:
            name, n = inc
            s = self.sem(name)
            self.val[name] += n
            self.q[engine].append(lambda eng, fn=fn, s=s, n=n: fn(eng).then_inc(s, n))

    def wait(self, engine, name, value=None):
        s = self.sem(name)
        v = self.val[name] if value is None else value
        if v <= 0:
            return
        self.q[engine].append(lambda eng, s=s, v=v: eng.wait_ge(s, v))

    def dma(self, engine, out, in_, sem="dma", **kw):
        s = self.sem(sem)
        self.val[sem] += 16
        self.q[engine].append(
            lambda eng, out=out, in_=in_, s=s, kw=kw:
            eng.dma_start(out=out, in_=in_, **kw).then_inc(s, 16))

    def barrier(self, name):
        sn = name + "_bar"
        self.sem(sn)
        target = self.val[sn] + len(ENGINES)
        dum = self.dummy
        self.op("sync", _nop, inc=(sn, 1))
        self.op("tensor", _nop, inc=(sn, 1))
        self.op("vector", lambda e: e.memset(dum[0:1, 0:4], 0.0), inc=(sn, 1))
        self.op("scalar", lambda e: e.copy(dum[0:1, 4:8], dum[0:1, 0:4]), inc=(sn, 1))
        self.op("gpsimd", lambda e: e.memset(dum[0:1, 8:12], 0.0), inc=(sn, 1))
        assert self.val[sn] == target
        for e in ENGINES:
            self.wait(e, sn, target)

    def replay(self):
        nc = self.nc
        with nc.Block() as block:
            @block.sync
            def _(eng):
                for f in self.q["sync"]:
                    f(eng)

            @block.vector
            def _(eng):
                for f in self.q["vector"]:
                    f(eng)

            @block.scalar
            def _(eng):
                for f in self.q["scalar"]:
                    f(eng)

            @block.tensor
            def _(eng):
                for f in self.q["tensor"]:
                    f(eng)

            @block.gpsimd
            def _(eng):
                for f in self.q["gpsimd"]:
                    f(eng)


class TransposePipe:

    def __init__(self, c, banks, ident, tag, evac_engine="scalar"):
        self.c = c
        self.banks = banks
        self.nb = len(banks)
        self.ident = ident
        self.tag = tag
        self.evac_engine = evac_engine
        self.t = 0
        self.tsem = "tr_" + tag
        self.esem = "tre_" + tag
        self.freed_at = []

    def emit(self, src_ap, dst_ap, extra_consumer=None):
        c = self.c
        bank = self.banks[self.t % self.nb]
        if self.t >= self.nb:
            c.wait("tensor", self.esem, self.freed_at[self.t - self.nb])
        c.op("tensor",
             lambda e, src_ap=src_ap, bank=bank: e.transpose(bank[:], src_ap, self.ident[:]),
             inc=(self.tsem, 1))
        c.wait(self.evac_engine, self.tsem, self.t + 1)
        if self.evac_engine == "scalar":
            c.op("scalar",
                 lambda e, dst_ap=dst_ap, bank=bank: e.copy(dst_ap, bank[:]),
                 inc=(self.esem, 1))
        else:
            c.op(self.evac_engine,
                 lambda e, dst_ap=dst_ap, bank=bank: e.tensor_copy(dst_ap, bank[:]),
                 inc=(self.esem, 1))
        if extra_consumer is not None:
            eng2, fn2 = extra_consumer
            c.wait(eng2, self.esem, c.val[self.esem])
            c.op(eng2, lambda e, bank=bank, fn2=fn2: fn2(e, bank), inc=(self.esem, 1))
        self.freed_at.append(c.val[self.esem])
        self.t += 1


def build_program(debug_outputs=(), phases=("a1", "a2", "a3", "b", "c")):
    nc = bass.Bass("TRN2", target_bir_lowering=False, detect_race_conditions=False)

    def scratch_kind(name):
        return "ExternalOutput" if name in debug_outputs else "Internal"

    x_d = nc.dram_tensor("x", [BB, S, D], F32, kind="ExternalInput").ap()
    wqkv_d = nc.dram_tensor("wqkv", [D, FQKV], F32, kind="ExternalInput").ap()
    bqkv_d = nc.dram_tensor("bqkv", [FQKV], F32, kind="ExternalInput").ap()
    wo_d = nc.dram_tensor("wo", [D, D], F32, kind="ExternalInput").ap()
    bo_d = nc.dram_tensor("bo", [D], F32, kind="ExternalInput").ap()
    rope_d = nc.dram_tensor("rope", [256, 32, 2], F32, kind="ExternalInput").ap()
    wg_d = nc.dram_tensor("wg", [I_FFN, D], F32, kind="ExternalInput").ap()
    wu_d = nc.dram_tensor("wu", [I_FFN, D], F32, kind="ExternalInput").ap()
    wd_d = nc.dram_tensor("wd", [D, I_FFN], F32, kind="ExternalInput").ap()
    out_d = nc.dram_tensor("out", [BB, S, D], F32, kind="ExternalOutput").ap()

    qt_s = nc.dram_tensor("qt_s", [BB, H, HD, S], F32, kind=scratch_kind("qt_s")).ap()
    kt_s = nc.dram_tensor("kt_s", [BB, H, HD, S], F32, kind=scratch_kind("kt_s")).ap()
    v_s = nc.dram_tensor("v_s", [BB, H, S, HD], F32, kind=scratch_kind("v_s")).ap()
    ao_s = nc.dram_tensor("ao_s", [BB, D, S], F32, kind=scratch_kind("ao_s")).ap()
    o1_s = nc.dram_tensor("o1_s", [BB, S, D], F32, kind=scratch_kind("o1_s")).ap()
    o2_s = nc.dram_tensor("o2_s", [BB, S, D], F32, kind=scratch_kind("o2_s")).ap()
    wq_dbg = nc.dram_tensor("wq_dbg", [6, 128, 11264], BF16, kind=scratch_kind("wq_dbg")).ap() \
        if "wq_dbg" in debug_outputs else None
    q_dbg = nc.dram_tensor("q_dbg", [2, 128, 1408], BF16, kind=scratch_kind("q_dbg")).ap() \
        if "q_dbg" in debug_outputs else None
    q_dbg2 = nc.dram_tensor("q_dbg2", [4, 128, 1408], F32, kind=scratch_kind("q_dbg2")).ap() \
        if "q_dbg2" in debug_outputs else None
    sw_dbg = nc.dram_tensor("sw_dbg", [4, 128, 1024], F32, kind=scratch_kind("sw_dbg")).ap() \
        if "sw_dbg" in debug_outputs else None
    ln_dbg = nc.dram_tensor("ln_dbg", [4, 128, NTILES], F32, kind=scratch_kind("ln_dbg")).ap() \
        if "ln_dbg" in debug_outputs else None
    hq_dbg = nc.dram_tensor("hq_dbg", [4, 128, 512], BF16, kind=scratch_kind("hq_dbg")).ap() \
        if "hq_dbg" in debug_outputs else None
    dp_s = nc.dram_tensor("dp_s", [BB, S, D], F32, kind=scratch_kind("dp_s")).ap()

    est = ExitStack()
    c = Ctx(nc, est)
    c.dummy = est.enter_context(nc.sbuf_tensor("barrier_dummy", [1, 12], F32))

    ident_f = est.enter_context(nc.sbuf_tensor("ident_f", [128, 128], F32))
    ident_b = est.enter_context(nc.sbuf_tensor("ident_b", [128, 128], BF16))
    mask_b = est.enter_context(nc.sbuf_tensor("mask_b", [128, 384], BF16))
    stats_m1 = est.enter_context(nc.sbuf_tensor("stats_m1", [128, NTILES], F32))
    stats_v1 = est.enter_context(nc.sbuf_tensor("stats_v1", [128, NTILES], F32))
    stats_m2 = est.enter_context(nc.sbuf_tensor("stats_m2", [128, NTILES], F32))
    stats_v2 = est.enter_context(nc.sbuf_tensor("stats_v2", [128, NTILES], F32))
    rstd1 = est.enter_context(nc.sbuf_tensor("rstd1", [128, NTILES], F32))
    rstd2 = est.enter_context(nc.sbuf_tensor("rstd2", [128, NTILES], F32))
    epsc = est.enter_context(nc.sbuf_tensor("epsc", [128, 1], F32))
    ubuf = est.enter_context(nc.sbuf_tensor("ubuf", [128, NTILES], F32))
    wbuf = est.enter_context(nc.sbuf_tensor("wbuf", [128, NTILES], F32))

    def emit_rstd(c, stats_m, stats_v, rstd_buf, fence_sem, fence_val):
        def vop(fn):
            c.op("vector", fn, inc=("v_rst", 1))
            c.wait("vector", "v_rst")
        c.wait("vector", fence_sem, fence_val)
        vop(lambda e: e.tensor_scalar(
            stats_m[:], stats_m[:], 2.0 ** -10, None, ALU.mult))
        vop(lambda e: e.tensor_tensor(wbuf[:], stats_m[:], stats_m[:], ALU.mult))
        vop(lambda e: e.tensor_scalar(
            ubuf[:], stats_v[:], 2.0 ** -10, EPS, ALU.mult, ALU.add))
        vop(lambda e: e.tensor_tensor(ubuf[:], ubuf[:], wbuf[:], ALU.subtract))
        c.op("vector", _nop, inc=("v_rst2", 1))
        c.wait("scalar", "v_rst2", c.val["v_rst2"])
        c.op("scalar", lambda e: e.activation(
            rstd_buf[:], ubuf[:], AF.Ln), inc=("s_rst", 1))
        c.wait("scalar", "s_rst")
        c.op("scalar", lambda e: e.activation(
            rstd_buf[:], rstd_buf[:], AF.Exp, scale=-0.5), inc=("s_rst", 1))
        c.wait("vector", "s_rst", c.val["s_rst"])
        vop(lambda e: e.tensor_tensor(wbuf[:], rstd_buf[:], rstd_buf[:], ALU.mult))
        vop(lambda e: e.tensor_tensor(wbuf[:], wbuf[:], ubuf[:], ALU.mult))
        vop(lambda e: e.tensor_scalar(
            wbuf[:], wbuf[:], -0.5, 1.5, ALU.mult, ALU.add))
        vop(lambda e: e.tensor_tensor(rstd_buf[:], rstd_buf[:], wbuf[:], ALU.mult))
        c.op("vector", _nop, inc=("v_rstd_done", 1))

    def phase_p0():
        c.op("gpsimd", lambda e: e.memset(ident_f[:], 1.0))
        c.op("gpsimd", lambda e: e.affine_select(
            ident_f[:], ident_f[:], [[-1, 128]], ALU.is_equal, 0.0,
            base=0, channel_multiplier=1))
        c.op("gpsimd", lambda e: e.memset(ident_b[:], 1.0))
        c.op("gpsimd", lambda e: e.affine_select(
            ident_b[:], ident_b[:], [[-1, 128]], ALU.is_equal, 0.0,
            base=0, channel_multiplier=1))
        c.op("gpsimd", lambda e: e.memset(epsc[:], EPS))
        c.op("gpsimd", lambda e: e.memset(mask_b[:], 0.0))
        c.op("gpsimd", lambda e: e.affine_select(
            mask_b[:, 0:128], mask_b[:, 0:128], [[1, 128]], ALU.is_ge, -1e30,
            base=0, channel_multiplier=-1))
        c.op("gpsimd", lambda e: e.affine_select(
            mask_b[:, 256:384], mask_b[:, 256:384], [[-1, 128]], ALU.is_ge, -1e30,
            base=0, channel_multiplier=1), inc=("g_const", 1))
        c.wait("tensor", "g_const", 1)
        c.wait("scalar", "g_const", 1)
        c.wait("vector", "g_const", 1)

    def phase_a1():
        with ExitStack() as ph:
            sb = lambda name, shape, dt=F32: ph.enter_context(nc.sbuf_tensor(name, shape, dt))
            wq_hi = sb("wq_hi", [128, 8, FQKV], BF16)
            wq_lo = sb("wq_lo", [128, 8, FQKV], BF16)
            wstage = [sb("wstage0", [128, FQKV])]
            brep = sb("brep", [128, FQKV])
            rop = sb("rop", [128, 128])
            tabs = {}
            for par in (0, 1):
                for cs in (0, 1):
                    for qk in (0, 1):
                        tabs[(par, cs, qk)] = sb(f"tab{par}{cs}{qk}", [128, 512])
            xt = [sb(f"xt{i}", [128, D]) for i in range(2)]
            xT_hi = sb("xT_hi", [128, D], BF16)
            xT_lo = sb("xT_lo", [128, D], BF16)
            xTf = sb("xTf", [128, D])
            qkv_sb = sb("qkv_sb", [128, FQKV])
            qr = sb("qr", [128, D])
            kr = sb("kr", [128, D])
            rtmp = sb("rtmp", [128, 512])
            qts = [sb(f"qts{i}", [128, D]) for i in range(2)]
            kts = [sb(f"kts{i}", [128, D]) for i in range(2)]
            vs_ = [sb(f"vs{i}", [128, D]) for i in range(2)]
            with ExitStack() as ps:
                pbanks = [ps.enter_context(nc.psum_tensor(f"ptrA1_{i}", [128, 128], F32))
                          for i in range(4)]
                pqkv = [ps.enter_context(nc.psum_tensor(f"pqkv{i}", [128, 512], F32))
                        for i in range(2)]
                pipe = TransposePipe(c, pbanks, ident_f, "a1")

                for fc in range(8):
                    buf = wstage[0]
                    if fc >= 1:
                        c.wait("sync", "v_wsplit", 2 * fc)
                    c.dma("sync", out=buf[:], in_=wqkv_d[fc * 128:(fc + 1) * 128, :], sem="ld_w")
                    c.wait("vector", "ld_w", (fc + 1) * 16)
                    c.op("vector", lambda e, fc=fc, buf=buf: e.tensor_copy(wq_hi[:, fc, :], buf[:]),
                         inc=("v_wsplit", 1))
                    c.op("vector", lambda e, fc=fc, buf=buf: e.tensor_tensor(
                        wq_lo[:, fc, :], buf[:], wq_hi[:, fc, :], ALU.subtract),
                        inc=("v_wsplit", 1))
                c.dma("sync", out=brep[:], in_=bqkv_d.unsqueeze(0).broadcast_to([128, FQKV]),
                      sem="ld_w")
                rflat = rope_d.rearrange("p j c -> p (j c)")
                c.dma("sync", out=rop[:, 0:64], in_=rflat[0:128, :], sem="ld_w")
                c.dma("sync", out=rop[:, 64:128], in_=rflat[128:256, :], sem="ld_w")
                c.wait("vector", "ld_w")
                for par in (0, 1):
                    for cs in (0, 1):
                        src = rop[:, 64 * par + cs: 64 * par + 64: 2]
                        src_b = src.unsqueeze(1).broadcast_to([128, H, 32])
                        for qk in (0, 1):
                            dst = tabs[(par, cs, qk)][:].rearrange("p (h j) -> p h j", j=32)
                            scale = 0.125 if qk == 0 else 1.0
                            c.op("vector", lambda e, dst=dst, src_b=src_b, scale=scale:
                                 e.tensor_scalar(dst, src_b, scale, None, ALU.mult),
                                 inc=("v_tab", 1))
                c.wait("tensor", "v_wsplit", 16)
                c.wait("tensor", "v_tab", 8)

                for it in range(NTILES):
                    b, ti = divmod(it, NT)
                    par = ti % 2
                    xb = xt[it % 2]
                    if it >= 2:
                        c.wait("sync", "t_xdone", it - 1)
                    c.dma("sync", out=xb[:], in_=x_d[b, ti * 128:(ti + 1) * 128, :], sem="ld_x")
                    c.wait("tensor", "ld_x", (it + 1) * 16)
                    for ch in range(8):
                        def lo_resid(e, bank, ch=ch):
                            return e.tensor_tensor(
                                xTf[:, ch * 128:(ch + 1) * 128], bank[:],
                                xT_hi[:, ch * 128:(ch + 1) * 128], ALU.subtract)
                        pipe.emit(xb[:, ch * 128:(ch + 1) * 128],
                                  xT_hi[:, ch * 128:(ch + 1) * 128],
                                  extra_consumer=("vector", lo_resid))
                        if ch == 7:
                            c.op("tensor", _nop, inc=("t_xdone", 1))
                    c.op("vector", lambda e: e.tensor_copy(xT_lo[:], xTf[:]), inc=("v_xlo", 1))

                    c.wait("tensor", "v_xlo", it + 1)
                    for fc in range(6):
                        pq = pqkv[fc % 2]
                        if it * 6 + fc >= 2:
                            c.wait("tensor", "v_qkvevac", it * 6 + fc - 1)
                        for ch in range(8):
                            c.op("tensor", lambda e, pq=pq, ch=ch, fc=fc: e.matmul(
                                pq[:], xT_hi[:, ch * 128:(ch + 1) * 128],
                                wq_hi[:, ch, fc * 512:(fc + 1) * 512],
                                start=(ch == 0), stop=False))
                            c.op("tensor", lambda e, pq=pq, ch=ch, fc=fc: e.matmul(
                                pq[:], xT_hi[:, ch * 128:(ch + 1) * 128],
                                wq_lo[:, ch, fc * 512:(fc + 1) * 512],
                                start=False, stop=False))
                            c.op("tensor", lambda e, pq=pq, ch=ch, fc=fc: e.matmul(
                                pq[:], xT_lo[:, ch * 128:(ch + 1) * 128],
                                wq_hi[:, ch, fc * 512:(fc + 1) * 512],
                                start=False, stop=(ch == 7)),
                                inc=(("t_qkv", 1) if ch == 7 else None))
                        c.wait("vector", "t_qkv", it * 6 + fc + 1)
                        if it >= 1 and fc == 0:
                            c.wait("vector", "s_vcopy", it)
                        c.op("vector", lambda e, pq=pq, fc=fc: e.tensor_tensor(
                            qkv_sb[:, fc * 512:(fc + 1) * 512], pq[:],
                            brep[:, fc * 512:(fc + 1) * 512], ALU.add),
                            inc=("v_qkvevac", 1))

                    if it >= 1:
                        c.wait("vector", "tr_a1", it * 24)
                    for qk, (dst, base) in enumerate(((qr, 0), (kr, D))):
                        x0 = qkv_sb[:, base + 0:base + D:2]
                        x1 = qkv_sb[:, base + 1:base + D:2]
                        ct = tabs[(par, 0, qk)][:]
                        st_ = tabs[(par, 1, qk)][:]
                        d0 = dst[:, 0:D:2]
                        d1 = dst[:, 1:D:2]
                        c.op("vector", lambda e, d0=d0, x0=x0, ct=ct: e.tensor_tensor(d0, x0, ct, ALU.mult))
                        c.op("vector", lambda e, x1=x1, st_=st_: e.tensor_tensor(rtmp[:], x1, st_, ALU.mult))
                        c.op("vector", lambda e, d0=d0: e.tensor_tensor(d0, d0, rtmp[:], ALU.subtract),
                             inc=("v_rope", 1))
                        c.op("vector", lambda e, d1=d1, x1=x1, ct=ct: e.tensor_tensor(d1, x1, ct, ALU.mult))
                        c.op("vector", lambda e, x0=x0, st_=st_: e.tensor_tensor(rtmp[:], x0, st_, ALU.mult))
                        c.op("vector", lambda e, d1=d1: e.tensor_tensor(d1, d1, rtmp[:], ALU.add),
                             inc=("v_rope", 1))

                    qb, kb, vb = qts[it % 2], kts[it % 2], vs_[it % 2]
                    if it >= 2:
                        c.wait("scalar", "st_qkt", (it - 1) * 48)
                    c.wait("tensor", "v_rope", (it + 1) * 4)
                    for src, dst in ((qr, qb), (kr, kb)):
                        for ch in range(8):
                            pipe.emit(src[:, ch * 128:(ch + 1) * 128],
                                      dst[:, ch * 128:(ch + 1) * 128])
                    c.wait("scalar", "v_qkvevac", (it + 1) * 6)
                    c.op("scalar", lambda e, vb=vb: e.copy(vb[:], qkv_sb[:, 2 * D:3 * D]),
                         inc=("s_vcopy", 1))
                    c.wait("sync", "tre_a1")
                    c.wait("sync", "s_vcopy", it + 1)
                    qdst = qt_s.rearrange("b h d s -> b (h d) s")[b, :, ti * 128:(ti + 1) * 128] \
                        .rearrange("(ch p) t -> p ch t", p=128)
                    kdst = kt_s.rearrange("b h d s -> b (h d) s")[b, :, ti * 128:(ti + 1) * 128] \
                        .rearrange("(ch p) t -> p ch t", p=128)
                    vdst = v_s[b, :, ti * 128:(ti + 1) * 128, :].transpose([1, 0, 2])
                    c.dma("sync", out=qdst, in_=qb[:].rearrange("p (ch t) -> p ch t", t=128), sem="st_qkt")
                    c.dma("sync", out=kdst, in_=kb[:].rearrange("p (ch t) -> p ch t", t=128), sem="st_qkt")
                    c.dma("sync", out=vdst, in_=vb[:].rearrange("p (h d) -> p h d", d=HD), sem="st_qkt")
                c.wait("sync", "st_qkt", c.val["st_qkt"])
        c.barrier("a1")

    def phase_a2():
        with ExitStack() as ph:
            sb = lambda name, shape, dt=F32: ph.enter_context(nc.sbuf_tensor(name, shape, dt))
            kt_sb = [sb(f"kt_sb{i}", [64, S]) for i in range(2)]
            qt_sb = [sb(f"qt_sb{i}", [64, S]) for i in range(2)]
            v_sb = [sb(f"v_sb{i}", [128, NT * HD]) for i in range(2)]
            junk = sb("junk", [128, 384])
            p_sb = sb("p_sb", [128, 384])
            pt_sb = sb("pt_sb", [128, 384])
            negm = [sb(f"negm{i}", [128, 1]) for i in range(2)]
            zrow = [sb(f"zrow{i}", [128, 1]) for i in range(2)]
            lnz = [sb(f"lnz{i}", [128, 1]) for i in range(2)]
            bias2 = [sb(f"bias2{i}", [128, 1]) for i in range(2)]
            aoh = [sb(f"aoh{i}", [64, S]) for i in range(2)]
            with ExitStack() as ps:
                pscore = [ps.enter_context(nc.psum_tensor(f"pscore{i}", [128, 384], F32))
                          for i in range(2)]
                ptr2 = [ps.enter_context(nc.psum_tensor(f"ptrA2_{i}", [128, 128], F32))
                        for i in range(3)]
                pao = [ps.enter_context(nc.psum_tensor(f"paoA2_{i}", [64, 128], F32))
                       for i in range(2)]
                pipe = TransposePipe(c, ptr2, ident_f, "a2", evac_engine="vector")

                def load_head(j):
                    b, h = divmod(j, H)
                    sem = f"ld_a2_{j % 2}"
                    if j >= 2:
                        c.wait("sync", "t_hd", j - 1)
                    c.dma("sync", out=kt_sb[j % 2][:], in_=kt_s[b, h], sem=sem)
                    c.dma("sync", out=qt_sb[j % 2][:], in_=qt_s[b, h], sem=sem)
                    c.dma("sync", out=v_sb[j % 2][:].rearrange("p (kt d) -> p kt d", d=HD),
                          in_=v_s[b, h].rearrange("(kt p) d -> p kt d", p=128), sem=sem)

                load_head(0)
                for j in range(BB * H):
                    b, h = divmod(j, H)
                    kt, qt, vt, ao = kt_sb[j % 2], qt_sb[j % 2], v_sb[j % 2], aoh[j % 2]
                    if j + 1 < BB * H:
                        load_head(j + 1)
                    c.wait("tensor", f"ld_a2_{j % 2}", (j // 2 + 1) * 48)
                    if j >= 2:
                        c.wait("scalar", "st_ao", (j - 1) * 16)
                    for i in range(NT):
                        g = j * NT + i
                        kt0 = max(i - 1, 0)
                        kt1 = min(i + 1, NT - 1)
                        nch = kt1 - kt0 + 1
                        kcols = nch * 128
                        psc = pscore[i % 2]
                        moff = 128 if i == 0 else 0
                        if g >= 2:
                            c.wait("tensor", "s_exp2", g - 1)
                        c.op("tensor", lambda e, psc=psc, qt=qt, i=i, kt=kt, kt0=kt0, kcols=kcols:
                             e.matmul(psc[:, 0:kcols], qt[:, i * 128:(i + 1) * 128],
                                      kt[:, kt0 * 128:kt0 * 128 + kcols],
                                      start=True, stop=False))
                        c.op("tensor", lambda e, psc=psc, kcols=kcols, moff=moff:
                             e.matmul(psc[:, 0:kcols], ident_b[:],
                                      mask_b[:, moff:moff + kcols],
                                      start=False, stop=True),
                             inc=("t_qk", 1))
                        nm, zr, lz, b2 = negm[g % 2], zrow[g % 2], lnz[g % 2], bias2[g % 2]
                        c.wait("vector", "t_qk", g + 1)
                        if g >= 1:
                            c.wait("vector", "s_exp1", g - 1)
                        c.op("vector", lambda e, psc=psc, kcols=kcols, nm=nm:
                             e.reduce_max(nm[:], psc[:, 0:kcols], AX.X, negate=True),
                             inc=("v_max", 1))
                        c.wait("scalar", "v_max", g + 1)
                        c.op("scalar", lambda e, psc=psc, kcols=kcols, nm=nm, zr=zr:
                             e.activation(junk[:, 0:kcols], psc[:, 0:kcols], AF.Exp,
                                          bias=nm[:, 0:1], accum_out=zr[:]), inc=("s_e1f", 1))
                        c.wait("scalar", "s_e1f")
                        c.op("scalar", lambda e, lz=lz, zr=zr: e.activation(lz[:], zr[:], AF.Ln),
                             inc=("s_exp1", 1))
                        c.wait("vector", "s_exp1", g + 1)
                        if g >= 1:
                            c.wait("vector", "s_exp2", g - 1)
                        c.op("vector", lambda e, b2=b2, nm=nm, lz=lz:
                             e.tensor_tensor(b2[:], nm[:], lz[:], ALU.subtract),
                             inc=("v_bias2", 1))
                        c.wait("scalar", "v_bias2", g + 1)
                        if g >= 1:
                            c.wait("scalar", "tr_a2", pipe.t)
                        c.op("scalar", lambda e, psc=psc, kcols=kcols, b2=b2:
                             e.activation(p_sb[:, 0:kcols], psc[:, 0:kcols], AF.Exp,
                                          bias=b2[:, 0:1]),
                             inc=("s_exp2", 1))
                        c.wait("tensor", "s_exp2", g + 1)
                        if g >= 1:
                            c.wait("tensor", "t_pv", g)
                        for ch in range(nch):
                            pipe.emit(p_sb[:, ch * 128:(ch + 1) * 128],
                                      pt_sb[:, ch * 128:(ch + 1) * 128])
                        pv = pao[i % 2]
                        if g >= 2:
                            c.wait("tensor", "s_ao", g - 1)
                        c.wait("tensor", "tre_a2", pipe.t)
                        for ch in range(nch):
                            ktile = kt0 + ch
                            c.op("tensor", lambda e, pv=pv, vt=vt, ktile=ktile, ch=ch, nch=nch:
                                 e.matmul(pv[:], vt[:, ktile * HD:(ktile + 1) * HD],
                                          pt_sb[:, ch * 128:(ch + 1) * 128],
                                          start=(ch == 0), stop=(ch == nch - 1)),
                                 inc=(("t_pv", 1) if ch == nch - 1 else None))
                        c.wait("scalar", "t_pv", g + 1)
                        c.op("scalar", lambda e, ao=ao, pv=pv, i=i:
                             e.copy(ao[:, i * 128:(i + 1) * 128], pv[:]),
                             inc=("s_ao", 1))
                    c.wait("sync", "s_ao", (j + 1) * NT)
                    c.dma("sync", out=ao_s[b, h * HD:(h + 1) * HD, :], in_=ao[:], sem="st_ao")
                    c.op("tensor", _nop, inc=("t_hd", 1))
                c.wait("sync", "st_ao", c.val["st_ao"])
        c.barrier("a2")

    def phase_a3():
        with ExitStack() as ph:
            sb = lambda name, shape, dt=F32: ph.enter_context(nc.sbuf_tensor(name, shape, dt))
            wo_hi = sb("wo_hi", [128, 8, D], BF16)
            wo_lo = sb("wo_lo", [128, 8, D], BF16)
            wstage3 = sb("wstage3", [128, D])
            borep = sb("borep", [128, D])
            aot = [sb(f"aot{i}", [128, D]) for i in range(2)]
            aot_hi = sb("aot_hi", [128, D], BF16)
            aot_lo = sb("aot_lo", [128, D], BF16)
            xt3 = [sb(f"xt3_{i}", [128, D]) for i in range(2)]
            o1t = [sb(f"o1t{i}", [128, D]) for i in range(2)]
            junk3 = sb("junk3", [128, D])
            junk3b = sb("junk3b", [128, D])
            acc3 = sb("acc3", [128, 2])
            nm3 = [sb(f"nm3_{i}", [128, 1]) for i in range(2)]
            with ExitStack() as ps:
                pwo = [ps.enter_context(nc.psum_tensor(f"pwo{i}", [128, 512], F32))
                       for i in range(2)]
                for fc in range(8):
                    if fc >= 1:
                        c.wait("sync", "v_wsplit3", 2 * fc)
                    c.dma("sync", out=wstage3[:], in_=wo_d[fc * 128:(fc + 1) * 128, :], sem="ld_w3")
                    c.wait("vector", "ld_w3", (fc + 1) * 16)
                    c.op("vector", lambda e, fc=fc: e.tensor_copy(wo_hi[:, fc, :], wstage3[:]),
                         inc=("v_wsplit3", 1))
                    c.op("vector", lambda e, fc=fc: e.tensor_tensor(
                        wo_lo[:, fc, :], wstage3[:], wo_hi[:, fc, :], ALU.subtract),
                        inc=("v_wsplit3", 1))
                c.dma("sync", out=borep[:], in_=bo_d.unsqueeze(0).broadcast_to([128, D]),
                      sem="ld_w3")
                c.wait("tensor", "v_wsplit3", 16)
                c.wait("vector", "ld_w3")

                for it in range(NTILES):
                    b, ti = divmod(it, NT)
                    ab, xb, ob = aot[it % 2], xt3[it % 2], o1t[it % 2]
                    if it >= 2:
                        c.wait("sync", "t_wo", (it - 1) * 2)
                    asrc = ao_s[b, :, ti * 128:(ti + 1) * 128].rearrange(
                        "(ch p) t -> p ch t", p=128)
                    c.dma("sync", out=ab[:].rearrange("p (ch t) -> p ch t", t=128),
                          in_=asrc, sem=f"ld_ao{it % 2}")
                    c.dma("sync", out=xb[:], in_=x_d[b, ti * 128:(ti + 1) * 128, :],
                          sem=f"ld_ao{it % 2}")
                    c.wait("scalar", f"ld_ao{it % 2}", (it // 2 + 1) * 32)
                    if it >= 1:
                        c.wait("scalar", "t_wo", it * 2)
                    c.op("scalar", lambda e, ab=ab: e.copy(aot_hi[:], ab[:]), inc=("s_ahi", 1))
                    c.wait("vector", "s_ahi", it + 1)
                    c.op("vector", lambda e, ab=ab: e.tensor_tensor(
                        aot_lo[:], ab[:], aot_hi[:], ALU.subtract), inc=("v_alo", 1))
                    c.wait("tensor", "v_alo", it + 1)
                    for fc in range(2):
                        pw = pwo[fc]
                        if it >= 1:
                            c.wait("tensor", "v_oevac", it * 2 + fc - 1)
                        for ch in range(8):
                            c.op("tensor", lambda e, pw=pw, ch=ch, fc=fc: e.matmul(
                                pw[:], aot_hi[:, ch * 128:(ch + 1) * 128],
                                wo_hi[:, ch, fc * 512:(fc + 1) * 512],
                                start=(ch == 0), stop=False))
                            c.op("tensor", lambda e, pw=pw, ch=ch, fc=fc: e.matmul(
                                pw[:], aot_hi[:, ch * 128:(ch + 1) * 128],
                                wo_lo[:, ch, fc * 512:(fc + 1) * 512],
                                start=False, stop=False))
                            c.op("tensor", lambda e, pw=pw, ch=ch, fc=fc: e.matmul(
                                pw[:], aot_lo[:, ch * 128:(ch + 1) * 128],
                                wo_hi[:, ch, fc * 512:(fc + 1) * 512],
                                start=False, stop=(ch == 7)),
                                inc=(("t_wo", 1) if ch == 7 else None))
                        c.wait("vector", "t_wo", it * 2 + fc + 1)
                        if it >= 2 and fc == 0:
                            c.wait("vector", "st_o1", (it - 1) * 16)
                        c.op("vector", lambda e, pw=pw, fc=fc: e.tensor_tensor(
                            junk3[:, fc * 512:(fc + 1) * 512], pw[:],
                            borep[:, fc * 512:(fc + 1) * 512], ALU.add))
                        c.op("vector", lambda e, ob=ob, xb=xb, fc=fc: e.tensor_tensor(
                            ob[:, fc * 512:(fc + 1) * 512],
                            junk3[:, fc * 512:(fc + 1) * 512],
                            xb[:, fc * 512:(fc + 1) * 512], ALU.add),
                            inc=("v_oevac", 1))
                    c.op("vector", lambda e, ob=ob, it=it: e.tensor_reduce(
                        stats_m1[:, it:it + 1].unsqueeze(2),
                        ob[:].unsqueeze(1), AX.X, ALU.add), inc=("v_red", 1))
                    c.op("vector", lambda e, ob=ob: e.tensor_tensor(
                        junk3b[:], ob[:], ob[:], ALU.mult))
                    c.op("vector", lambda e, it=it: e.tensor_reduce(
                        stats_v1[:, it:it + 1].unsqueeze(2),
                        junk3b[:].unsqueeze(1), AX.X, ALU.add), inc=("v_red", 1))
                    c.wait("sync", "v_oevac", (it + 1) * 2)
                    c.dma("sync", out=o1_s[b, ti * 128:(ti + 1) * 128, :], in_=ob[:],
                          sem="st_o1")
                emit_rstd(c, stats_m1, stats_v1, rstd1, "v_red", None)
                c.op("vector", _nop, inc=("v_m1", 1))
                if "ln_dbg" in debug_outputs:
                    c.wait("sync", "v_rstd_done", 1)
                    c.wait("sync", "v_m1", 1)
                    c.dma("sync", out=ln_dbg[0, :, :], in_=stats_m1[:], sem="st_dbg")
                    c.dma("sync", out=ln_dbg[1, :, :], in_=stats_v1[:], sem="st_dbg")
                    c.dma("sync", out=ln_dbg[2, :, :], in_=ubuf[:], sem="st_dbg")
                    c.dma("sync", out=ln_dbg[3, :, :], in_=rstd1[:], sem="st_dbg")
                    c.wait("sync", "st_dbg", c.val["st_dbg"])
                c.wait("sync", "st_o1", c.val["st_o1"])
        c.barrier("a3")

    def phase_b():
        QN = 1408
        with ExitStack() as ph:
            sb = lambda name, shape, dt=F32: ph.enter_context(nc.sbuf_tensor(name, shape, dt))
            wgT_hi = sb("wgT_hi", [128, 8 * I_HALF], BF16)
            wgT_lo = sb("wgT_lo", [128, 8 * I_HALF], BF16)
            wuT_hi = sb("wuT_hi", [128, 8 * I_HALF], BF16)
            wuT_lo = sb("wuT_lo", [128, 8 * I_HALF], BF16)
            wdT_hi = sb("wdT_hi", [128, 11 * D], BF16)
            wdT_lo = sb("wdT_lo", [128, 11 * D], BF16)
            qa = sb("qa", [128, QN])
            qsgn = sb("qsgn", [128, QN], BF16)
            qssg = sb("qssg", [128, QN])
            qcA = sb("qcA", [128, QN], BF16)
            qcB = sb("qcB", [128, QN], BF16)
            qmr = sb("qmr", [128, 128])
            qs_ = sb("qs_", [128, 128])
            qsx = sb("qsx", [128, QN])
            qh_ = sb("qh_", [128, QN], BF16)
            ql_ = sb("ql_", [128, QN], BF16)
            wstg = [sb("wstgB0", [128, QN])]
            o1b = [sb(f"o1b{i}", [128, D]) for i in range(2)]
            xqT_hi = sb("xqT_hi", [128, D], BF16)
            xqT_lo = sb("xqT_lo", [128, D], BF16)
            hbuf = sb("hbuf", [128, 512])
            gact = sb("gact", [128, 512])
            hqT_hi = sb("hqT_hi", [128, 512], BF16)
            hqT_lo = sb("hqT_lo", [128, 512], BF16)
            dpt = [sb(f"dpt{i}", [128, D]) for i in range(2)]
            o2t = [sb(f"o2t{i}", [128, D]) for i in range(2)]
            accb = sb("accb", [128, 2])
            nm2 = [sb(f"nm2_{i}", [128, 1]) for i in range(2)]

            def emit_quant(src_ap, n):
                nb = n // 16
                a3d = qa[:, 0:n].rearrange("p (nb w) -> p nb w", w=16)
                sb3 = qs_[:, 0:nb].unsqueeze(2).broadcast_to([128, nb, 16])
                sx3 = qsx[:, 0:n].rearrange("p (nb w) -> p nb w", w=16)

                def vop(fn):
                    c.op("vector", fn, inc=("q_v", 1))
                    c.wait("vector", "q_v")

                c.wait("scalar", "q_v")
                c.op("scalar", lambda e, src_ap=src_ap, n=n: e.activation(
                    qa[:, 0:n], src_ap, AF.Abs), inc=("q_s", 1))
                c.wait("scalar", "q_s")
                c.op("scalar", lambda e, src_ap=src_ap, n=n: e.activation(
                    qsgn[:, 0:n], src_ap, AF.Sign), inc=("q_s", 1))
                c.wait("vector", "q_s")
                vop(lambda e, a3d=a3d, nb=nb: e.tensor_reduce(
                    qmr[:, 0:nb].unsqueeze(2), a3d, AX.X, ALU.max))
                vop(lambda e, nb=nb: e.tensor_scalar(
                    qs_[:, 0:nb], qmr[:, 0:nb], 1.0 / 6.0, 1e-12, ALU.mult, ALU.max))
                vop(lambda e, sx3=sx3, sb3=sb3: e.tensor_scalar(
                    sx3, sb3, 1.0, None, ALU.mult))
                vop(lambda e, n=n: e.scalar_tensor_tensor(
                    qcA[:, 0:n], qsx[:, 0:n], MIDS[0], qa[:, 0:n], ALU.mult, ALU.is_lt))
                for j in (1, 2, 3):
                    vop(lambda e, n=n, j=j: e.scalar_tensor_tensor(
                        qcB[:, 0:n], qsx[:, 0:n], MIDS[j], qa[:, 0:n], ALU.mult, ALU.is_lt))
                    vop(lambda e, n=n: e.tensor_tensor(
                        qcA[:, 0:n], qcA[:, 0:n], qcB[:, 0:n], ALU.add))
                vop(lambda e, n=n: e.scalar_tensor_tensor(
                    qssg[:, 0:n], qsx[:, 0:n], MIDS[4], qa[:, 0:n], ALU.mult, ALU.is_lt))
                vop(lambda e, n=n: e.scalar_tensor_tensor(
                    qcB[:, 0:n], qsx[:, 0:n], MIDS[5], qa[:, 0:n], ALU.mult, ALU.is_lt))
                vop(lambda e, n=n: e.tensor_tensor(
                    qssg[:, 0:n], qssg[:, 0:n], qcB[:, 0:n], ALU.add))
                vop(lambda e, n=n: e.scalar_tensor_tensor(
                    qcA[:, 0:n], qcA[:, 0:n], 0.5, qssg[:, 0:n], ALU.mult, ALU.add))
                vop(lambda e, n=n: e.scalar_tensor_tensor(
                    qcB[:, 0:n], qsx[:, 0:n], MIDS[6], qa[:, 0:n], ALU.mult, ALU.is_lt))
                vop(lambda e, n=n: e.scalar_tensor_tensor(
                    qcA[:, 0:n], qcB[:, 0:n], 2.0, qcA[:, 0:n], ALU.mult, ALU.add))
                vop(lambda e, n=n: e.tensor_tensor(
                    qssg[:, 0:n], qsgn[:, 0:n], qsx[:, 0:n], ALU.mult))
                vop(lambda e, n=n: e.tensor_tensor(
                    qssg[:, 0:n], qcA[:, 0:n], qssg[:, 0:n], ALU.mult))
                vop(lambda e, n=n: e.tensor_copy(qh_[:, 0:n], qssg[:, 0:n]))
                vop(lambda e, n=n: e.tensor_tensor(
                    ql_[:, 0:n], qssg[:, 0:n], qh_[:, 0:n], ALU.subtract))

            with ExitStack() as ps:
                pd = [ps.enter_context(nc.psum_tensor(f"pdn{i}", [128, D], F32))
                      for i in range(2)]
                pg = ps.enter_context(nc.psum_tensor("pgate", [128, 512], F32))
                pu = ps.enter_context(nc.psum_tensor("pup", [128, 512], F32))
                ptrB = [ps.enter_context(nc.psum_tensor(f"ptrB{i}", [128, 128], BF16))
                        for i in range(2)]
                pipe = TransposePipe(c, ptrB, ident_b, "b")

                W_SUB = [512] if "onesub" in debug_outputs else [512, 512, 384]
                x1_qs = {}
                for half in range(2):
                    if half == 1:
                        c.barrier("b_half")
                    wt_jobs = []
                    for rt in range(11):
                        wt_jobs.append(("g", rt, wg_d[half * I_HALF + rt * 128:
                                                     half * I_HALF + (rt + 1) * 128, :], D))
                    for rt in range(11):
                        wt_jobs.append(("u", rt, wu_d[half * I_HALF + rt * 128:
                                                     half * I_HALF + (rt + 1) * 128, :], D))
                    for rt in range(8):
                        wt_jobs.append(("d", rt, wd_d[rt * 128:(rt + 1) * 128,
                                                      half * I_HALF:(half + 1) * I_HALF], I_HALF))
                    qs_after = []
                    for jn, (kind, rt, src, n) in enumerate(wt_jobs):
                        buf = wstg[0]
                        if jn >= 1:
                            c.wait("sync", "q_s", qs_after[jn - 1])
                        c.dma("sync", out=buf[:, 0:n], in_=src, sem="ld_wb")
                        c.wait("scalar", "ld_wb", c.val["ld_wb"])
                        emit_quant(buf[:, 0:n], n)
                        qs_after.append(c.val["q_s"])
                        c.wait("tensor", "q_v")
                        if q_dbg2 is not None and half == 0 and jn == 0:
                            c.wait("sync", "q_v")
                            c.dma("sync", out=q_dbg2[0, :, 0:128], in_=qmr[:], sem="st_dbg")
                            c.dma("sync", out=q_dbg2[1, :, 0:128], in_=qs_[:], sem="st_dbg")
                            c.dma("sync", out=q_dbg2[2, :, 0:128], in_=qr_[:], sem="st_dbg")
                            c.dma("sync", out=q_dbg2[3, :, 0:n], in_=qa[:, 0:n], sem="st_dbg")
                            c.wait("sync", "st_dbg", c.val["st_dbg"])
                            c.barrier("dbgb")
                        if q_dbg is not None and half == 0 and jn == 0:
                            c.wait("sync", "q_v")
                            c.dma("sync", out=q_dbg[0, :, 0:n], in_=qh_[:, 0:n], sem="st_dbg")
                            c.dma("sync", out=q_dbg[1, :, 0:n], in_=ql_[:, 0:n], sem="st_dbg")
                            c.wait("sync", "st_dbg", 32)
                        nch = n // 128
                        for ch in range(nch):
                            for hl, (qsrc, wdst) in enumerate((
                                    (qh_, {"g": wgT_hi, "u": wuT_hi, "d": wdT_hi}[kind]),
                                    (ql_, {"g": wgT_lo, "u": wuT_lo, "d": wdT_lo}[kind]))):
                                if kind == "d":
                                    dst = wdst[:, ch * D + rt * 128: ch * D + (rt + 1) * 128]
                                else:
                                    dst = wdst[:, ch * I_HALF + rt * 128: ch * I_HALF + (rt + 1) * 128]
                                pipe.emit(qsrc[:, ch * 128:(ch + 1) * 128], dst)
                        c.wait("vector", "tr_b", pipe.t)
                    c.wait("tensor", "tre_b")
                    if wq_dbg is not None and half == 0:
                        c.wait("sync", "tre_b")
                        for di, wt in enumerate((wgT_hi, wgT_lo, wuT_hi, wuT_lo,
                                                 wdT_hi, wdT_lo)):
                            c.dma("sync", out=wq_dbg[di], in_=wt[:], sem="st_dbg")
                        c.wait("sync", "st_dbg", 96)

                    for it in range(NTILES):
                        b, ti = divmod(it, NT)
                        ob = o1b[it % 2]
                        xb = ob
                        pdn = pd[it % 2]
                        gidx = half * NTILES + it
                        if gidx >= 2:
                            c.wait("sync", "q_s", x1_qs.get(gidx - 2, 0))
                            if half == 1 and it >= 2:
                                c.wait("sync", "v_o2", (it - 1) * 2)
                        c.dma("sync", out=ob[:], in_=o1_s[b, ti * 128:(ti + 1) * 128, :],
                              sem=f"ld_o1{it % 2}")
                        if half == 1:
                            db = dpt[it % 2]
                            if it >= 2:
                                c.wait("sync", "v_o2", (it - 1) * 2)
                            c.dma("sync", out=db[:], in_=dp_s[b, ti * 128:(ti + 1) * 128, :],
                                  sem=f"ld_dp{it % 2}")
                        if gidx == 0:
                            c.wait("vector", "v_rstd_done", 1)
                            c.wait("vector", "v_m1", 1)
                        c.wait("vector", f"ld_o1{it % 2}", ((half * NTILES + it) // 2 + 1) * 16)
                        c.op("vector", lambda e, ob=ob, it=it: e.tensor_scalar(
                            ob[:], ob[:], stats_m1[:, it:it + 1], rstd1[:, it:it + 1],
                            ALU.subtract, ALU.mult), inc=("v_x1", 1))
                        c.wait("scalar", "v_x1", gidx + 1)
                        emit_quant(xb[:], D)
                        x1_qs[gidx] = c.val["q_s"]
                        c.wait("tensor", "q_v")
                        for ch in range(8):
                            pipe.emit(qh_[:, ch * 128:(ch + 1) * 128],
                                      xqT_hi[:, ch * 128:(ch + 1) * 128])
                            pipe.emit(ql_[:, ch * 128:(ch + 1) * 128],
                                      xqT_lo[:, ch * 128:(ch + 1) * 128])
                        c.wait("vector", "tr_b", pipe.t)
                        if gidx >= 2:
                            if half == 0:
                                c.wait("tensor", "s_dpe", (gidx - 1) * 2)
                            elif it >= 2:
                                c.wait("tensor", "v_o2", (gidx - NTILES - 1) * 2)
                        icol = 0
                        for sub, wsub in enumerate(W_SUB):
                            c.wait("tensor", "tre_b")
                            nsub = len(W_SUB)
                            if gidx * nsub + sub >= 1:
                                c.wait("tensor", "s_silu", gidx * nsub + sub)
                                c.wait("tensor", "v_h", gidx * nsub + sub)
                            for ch in range(8):
                                co = ch * I_HALF + icol
                                c.op("tensor", lambda e, ch=ch, co=co, wsub=wsub: e.matmul(
                                    pg[:, 0:wsub], xqT_hi[:, ch * 128:(ch + 1) * 128],
                                    wgT_hi[:, co:co + wsub], start=(ch == 0), stop=False))
                                c.op("tensor", lambda e, ch=ch, co=co, wsub=wsub: e.matmul(
                                    pg[:, 0:wsub], xqT_hi[:, ch * 128:(ch + 1) * 128],
                                    wgT_lo[:, co:co + wsub], start=False, stop=False))
                                c.op("tensor", lambda e, ch=ch, co=co, wsub=wsub: e.matmul(
                                    pg[:, 0:wsub], xqT_lo[:, ch * 128:(ch + 1) * 128],
                                    wgT_hi[:, co:co + wsub], start=False, stop=(ch == 7)),
                                    inc=(("t_g", 1) if ch == 7 else None))
                            for ch in range(8):
                                co = ch * I_HALF + icol
                                c.op("tensor", lambda e, ch=ch, co=co, wsub=wsub: e.matmul(
                                    pu[:, 0:wsub], xqT_hi[:, ch * 128:(ch + 1) * 128],
                                    wuT_hi[:, co:co + wsub], start=(ch == 0), stop=False))
                                c.op("tensor", lambda e, ch=ch, co=co, wsub=wsub: e.matmul(
                                    pu[:, 0:wsub], xqT_hi[:, ch * 128:(ch + 1) * 128],
                                    wuT_lo[:, co:co + wsub], start=False, stop=False))
                                c.op("tensor", lambda e, ch=ch, co=co, wsub=wsub: e.matmul(
                                    pu[:, 0:wsub], xqT_lo[:, ch * 128:(ch + 1) * 128],
                                    wuT_hi[:, co:co + wsub], start=False, stop=(ch == 7)),
                                    inc=(("t_u", 1) if ch == 7 else None))
                            if sw_dbg is not None and gidx == DBG_GIDX and sub == 0:
                                c.wait("sync", "v_x1", 1)
                                c.dma("sync", out=sw_dbg[0], in_=xb[:], sem="st_dbg")
                                c.wait("vector", "t_g", 1)
                                c.op("vector", lambda e: e.tensor_copy(qssg[:, 0:512], pg[:, 0:512]),
                                     inc=("v_dbg", 1))
                                c.wait("sync", "v_dbg", 1)
                                c.dma("sync", out=sw_dbg[1, :, 0:512], in_=qssg[:, 0:512], sem="st_dbg")
                                c.wait("vector", "t_u", 1)
                                c.op("vector", lambda e: e.tensor_copy(qssg[:, 512:1024], pu[:, 0:512]),
                                     inc=("v_dbg", 1))
                                c.wait("sync", "v_dbg", 2)
                                c.dma("sync", out=sw_dbg[2, :, 0:512], in_=qssg[:, 512:1024], sem="st_dbg")
                                c.wait("sync", "st_dbg", c.val["st_dbg"])
                                c.barrier("dbg2")
                            c.wait("scalar", "t_g", gidx * nsub + sub + 1)
                            if sw_dbg is not None and gidx == DBG_GIDX and sub == 0:
                                dbg_h = True
                            else:
                                dbg_h = False
                            c.wait("scalar", "v_h")
                            c.op("scalar", lambda e, wsub=wsub: e.activation(
                                gact[:, 0:wsub], pg[:, 0:wsub], AF.Silu), inc=("s_silu", 1))
                            c.wait("vector", "t_u", gidx * nsub + sub + 1)
                            c.wait("vector", "s_silu", gidx * nsub + sub + 1)
                            c.wait("vector", "q_s")
                            c.op("vector", lambda e, wsub=wsub: e.tensor_tensor(
                                hbuf[:, 0:wsub], gact[:, 0:wsub], pu[:, 0:wsub], ALU.mult),
                                inc=("v_h", 1))
                            if dbg_h:
                                c.wait("sync", "v_h", 1)
                                c.dma("sync", out=sw_dbg[3, :, 0:wsub], in_=hbuf[:, 0:wsub],
                                      sem="st_dbg")
                                c.wait("sync", "st_dbg", c.val["st_dbg"])
                                c.barrier("dbg3")
                            c.wait("scalar", "v_h")
                            emit_quant(hbuf[:, 0:wsub], wsub)
                            if dbg_h and q_dbg2 is not None:
                                c.wait("sync", "q_v")
                                c.dma("sync", out=q_dbg2[0, :, 0:128], in_=qmr[:], sem="st_dbg")
                                c.dma("sync", out=q_dbg2[1, :, 0:128], in_=qs_[:], sem="st_dbg")
                                c.dma("sync", out=q_dbg2[2, :, 0:128], in_=qr_[:], sem="st_dbg")
                                c.dma("sync", out=q_dbg2[3, :, 0:wsub], in_=qa[:, 0:wsub], sem="st_dbg")
                                c.wait("sync", "st_dbg", c.val["st_dbg"])
                                c.barrier("dbg5")
                            nci = wsub // 128
                            c.wait("tensor", "q_v")
                            for ci in range(nci):
                                pipe.emit(qh_[:, ci * 128:(ci + 1) * 128],
                                          hqT_hi[:, ci * 128:(ci + 1) * 128])
                                pipe.emit(ql_[:, ci * 128:(ci + 1) * 128],
                                          hqT_lo[:, ci * 128:(ci + 1) * 128])
                            if hq_dbg is not None and gidx == DBG_GIDX and sub == 0:
                                c.wait("sync", "q_v")
                                c.dma("sync", out=hq_dbg[0, :, 0:wsub], in_=qh_[:, 0:wsub], sem="st_dbg")
                                c.dma("sync", out=hq_dbg[1, :, 0:wsub], in_=ql_[:, 0:wsub], sem="st_dbg")
                                c.wait("sync", "tre_b")
                                c.dma("sync", out=hq_dbg[2, :, 0:wsub], in_=hqT_hi[:, 0:wsub], sem="st_dbg")
                                c.dma("sync", out=hq_dbg[3, :, 0:wsub], in_=hqT_lo[:, 0:wsub], sem="st_dbg")
                                c.wait("sync", "st_dbg", c.val["st_dbg"])
                                c.barrier("dbg4")
                            c.wait("vector", "tr_b", pipe.t)
                            c.wait("tensor", "tre_b")
                            first = (sub == 0)
                            last = (sub == len(W_SUB) - 1)
                            for ci in range(nci):
                                cig = (icol // 128) + ci
                                for hc in range(2):
                                    for ps_, (lhs, rhs) in enumerate((
                                            (hqT_hi, wdT_hi), (hqT_hi, wdT_lo), (hqT_lo, wdT_hi))):
                                        st = first and ci == 0 and ps_ == 0
                                        sp = last and ci == nci - 1 and ps_ == 2
                                        c.op("tensor", lambda e, pdn=pdn, lhs=lhs, rhs=rhs,
                                             ci=ci, cig=cig, hc=hc, st=st, sp=sp: e.matmul(
                                                 pdn[:, hc * 512:(hc + 1) * 512],
                                                 lhs[:, ci * 128:(ci + 1) * 128],
                                                 rhs[:, cig * D + hc * 512:cig * D + (hc + 1) * 512],
                                                 start=st, stop=sp),
                                             inc=(("t_dn", 1) if (sp and hc == 1) else None))
                            icol += wsub
                        if half == 0:
                            dpo = dpt[it % 2]
                            if it >= 2:
                                c.wait("scalar", "st_dp", (it - 1) * 16)
                            c.wait("scalar", "t_dn", gidx + 1)
                            c.op("scalar", lambda e, dpo=dpo, pdn=pdn: e.copy(
                                dpo[:, 0:512], pdn[:, 0:512]), inc=("s_dpe", 1))
                            c.op("scalar", lambda e, dpo=dpo, pdn=pdn: e.copy(
                                dpo[:, 512:D], pdn[:, 512:D]), inc=("s_dpe", 1))
                            c.wait("sync", "s_dpe", (it + 1) * 2)
                            c.dma("sync", out=dp_s[b, ti * 128:(ti + 1) * 128, :],
                                  in_=dpo[:], sem="st_dp")
                        else:
                            o2b = o2t[it % 2]
                            nm = nm2[it % 2]
                            c.wait("vector", "t_dn", gidx + 1)
                            c.wait("vector", f"ld_dp{it % 2}", (it // 2 + 1) * 16)
                            if it >= 2:
                                c.wait("vector", "st_o2", (it - 1) * 16)
                                c.wait("vector", "s_sq2", it - 1)
                            dbl = dpt[it % 2]
                            for hc in range(2):
                                c.op("vector", lambda e, pdn=pdn, dbl=dbl, o2b=o2b, hc=hc:
                                     e.tensor_tensor(
                                         o2b[:, hc * 512:(hc + 1) * 512],
                                         pdn[:, hc * 512:(hc + 1) * 512],
                                         dbl[:, hc * 512:(hc + 1) * 512], ALU.add))
                                c.op("vector", lambda e, o2b=o2b, xb=xb, hc=hc: e.tensor_tensor(
                                    o2b[:, hc * 512:(hc + 1) * 512],
                                    o2b[:, hc * 512:(hc + 1) * 512],
                                    xb[:, hc * 512:(hc + 1) * 512], ALU.add),
                                    inc=("v_o2", 1))
                            c.op("vector", lambda e, o2b=o2b, it=it: e.tensor_reduce(
                                stats_m2[:, it:it + 1].unsqueeze(2),
                                o2b[:].unsqueeze(1), AX.X, ALU.add), inc=("v_red", 1))
                            c.op("vector", lambda e, o2b=o2b: e.tensor_tensor(
                                qsx[:, 0:D], o2b[:], o2b[:], ALU.mult))
                            c.op("vector", lambda e, it=it: e.tensor_reduce(
                                stats_v2[:, it:it + 1].unsqueeze(2),
                                qsx[:, 0:D].unsqueeze(1), AX.X, ALU.add), inc=("v_red", 1))
                            c.op("vector", _nop, inc=("s_sq2", 1))
                            c.wait("sync", "v_o2", c.val["v_o2"])
                            c.dma("sync", out=o2_s[b, ti * 128:(ti + 1) * 128, :],
                                  in_=o2b[:], sem="st_o2")
                c.wait("sync", "st_o2", c.val["st_o2"])
        c.barrier("b")

    def phase_c():
        with ExitStack() as ph:
            sb = lambda name, shape, dt=F32: ph.enter_context(nc.sbuf_tensor(name, shape, dt))
            o2c = [sb(f"o2c{i}", [128, D]) for i in range(2)]
            oc = [sb(f"oc{i}", [128, D]) for i in range(2)]
            emit_rstd(c, stats_m2, stats_v2, rstd2, "v_red", None)
            c.op("vector", _nop, inc=("v_m2", 1))
            c.wait("vector", "v_rstd_done", 2)
            for it in range(NTILES):
                b, ti = divmod(it, NT)
                ib, ub = o2c[it % 2], oc[it % 2]
                if it >= 2:
                    c.wait("sync", "v_out", it - 1)
                c.dma("sync", out=ib[:], in_=o2_s[b, ti * 128:(ti + 1) * 128, :],
                      sem=f"ld_oc{it % 2}")
                c.wait("vector", f"ld_oc{it % 2}", (it // 2 + 1) * 16)
                if it >= 2:
                    c.wait("vector", "st_out", (it - 1) * 16)
                c.op("vector", lambda e, ib=ib, ub=ub, it=it: e.tensor_scalar(
                    ub[:], ib[:], stats_m2[:, it:it + 1], rstd2[:, it:it + 1],
                    ALU.subtract, ALU.mult), inc=("v_out", 1))
                c.wait("sync", "v_out", it + 1)
                c.dma("sync", out=out_d[b, ti * 128:(ti + 1) * 128, :], in_=ub[:],
                      sem="st_out")
            c.wait("sync", "st_out", c.val["st_out"])

    phase_p0()
    if "a1" in phases:
        phase_a1()
    if "a2" in phases:
        phase_a2()
    if "a3" in phases:
        phase_a3()
    if "b" in phases:
        phase_b()
    if "c" in phases:
        phase_c()
    c.replay()
    est.close()
    return nc


_NC_CACHE = {}


def _get_nc():
    if "nc" not in _NC_CACHE:
        _NC_CACHE["nc"] = build_program()
    return _NC_CACHE["nc"]


def kernel(L0_attn_hidden_states, L0_attn_Wqkv_kernel, L0_attn_Wqkv_bias,
           L0_attn_Wo_kernel, L0_attn_Wo_bias, L0_attn_rope_cache,
           L0_ffn_w_gate, L0_ffn_w_up, L0_ffn_w_down):
    x = np.ascontiguousarray(np.asarray(L0_attn_hidden_states, dtype=np.float32))
    common = {
        "wqkv": np.ascontiguousarray(np.asarray(L0_attn_Wqkv_kernel, np.float32)),
        "bqkv": np.ascontiguousarray(np.asarray(L0_attn_Wqkv_bias, np.float32)),
        "wo": np.ascontiguousarray(np.asarray(L0_attn_Wo_kernel, np.float32)),
        "bo": np.ascontiguousarray(np.asarray(L0_attn_Wo_bias, np.float32)),
        "rope": np.ascontiguousarray(np.asarray(L0_attn_rope_cache, np.float32)),
        "wg": np.ascontiguousarray(np.asarray(L0_ffn_w_gate, np.float32)),
        "wu": np.ascontiguousarray(np.asarray(L0_ffn_w_up, np.float32)),
        "wd": np.ascontiguousarray(np.asarray(L0_ffn_w_down, np.float32)),
    }
    nc = _get_nc()
    in_maps = [dict(common, x=x[i * BB:(i + 1) * BB]) for i in range(NCORES)]
    res = run_bass_kernel_spmd(nc, in_maps, core_ids=list(range(NCORES)))
    return np.concatenate([r["out"] for r in res.results], axis=0)


# version 05 (final)
# speedup vs baseline: 1.0684x; 1.0684x over previous
import numpy as np
from contextlib import ExitStack

import concourse.bass as bass
import concourse.mybir as mybir
from concourse.bass_utils import run_bass_kernel_spmd

F32 = mybir.dt.float32
BF16 = mybir.dt.bfloat16
ALU = mybir.AluOpType
AF = mybir.ActivationFunctionType
AX = mybir.AxisListType

NCORES = 8
B_FULL = 16
BB = B_FULL // NCORES
S = 2048
D = 1024
H = 16
HD = 64
FQKV = 3 * D
NT = S // 128
NTILES = BB * NT
I_FFN = 2816
I_HALF = I_FFN // 2
EPS = 1e-5

MIDS = [0.25, 0.75, 1.25, 1.75, 2.5, 3.5, 5.0]
W2 = [0.5, 0.5, 0.5, 0.5, 1.0, 1.0, 2.0]

ENGINES = ("sync", "vector", "scalar", "tensor", "gpsimd")
DBG_GIDX = 0


def _nop(eng):
    return eng.nop()


class Ctx:

    def __init__(self, nc, est):
        self.nc = nc
        self.est = est
        self.q = {e: [] for e in ENGINES}
        self.sems = {}
        self.val = {}

    def sem(self, name):
        if name not in self.sems:
            self.sems[name] = self.est.enter_context(self.nc.semaphore(name))
            self.val[name] = 0
        return self.sems[name]

    def op(self, engine, fn, inc=None):
        if inc is None:
            self.q[engine].append(fn)
        else:
            name, n = inc
            s = self.sem(name)
            self.val[name] += n
            self.q[engine].append(lambda eng, fn=fn, s=s, n=n: fn(eng).then_inc(s, n))

    def wait(self, engine, name, value=None):
        s = self.sem(name)
        v = self.val[name] if value is None else value
        if v <= 0:
            return
        self.q[engine].append(lambda eng, s=s, v=v: eng.wait_ge(s, v))

    def dma(self, engine, out, in_, sem="dma", **kw):
        s = self.sem(sem)
        self.val[sem] += 16
        self.q[engine].append(
            lambda eng, out=out, in_=in_, s=s, kw=kw:
            eng.dma_start(out=out, in_=in_, **kw).then_inc(s, 16))

    def barrier(self, name):
        sn = name + "_bar"
        self.sem(sn)
        target = self.val[sn] + len(ENGINES)
        dum = self.dummy
        self.op("sync", _nop, inc=(sn, 1))
        self.op("tensor", _nop, inc=(sn, 1))
        self.op("vector", lambda e: e.memset(dum[0:1, 0:4], 0.0), inc=(sn, 1))
        self.op("scalar", lambda e: e.copy(dum[0:1, 4:8], dum[0:1, 0:4]), inc=(sn, 1))
        self.op("gpsimd", lambda e: e.memset(dum[0:1, 8:12], 0.0), inc=(sn, 1))
        assert self.val[sn] == target
        for e in ENGINES:
            self.wait(e, sn, target)

    def replay(self):
        nc = self.nc
        with nc.Block() as block:
            @block.sync
            def _(eng):
                for f in self.q["sync"]:
                    f(eng)

            @block.vector
            def _(eng):
                for f in self.q["vector"]:
                    f(eng)

            @block.scalar
            def _(eng):
                for f in self.q["scalar"]:
                    f(eng)

            @block.tensor
            def _(eng):
                for f in self.q["tensor"]:
                    f(eng)

            @block.gpsimd
            def _(eng):
                for f in self.q["gpsimd"]:
                    f(eng)


class TransposePipe:

    def __init__(self, c, banks, ident, tag, evac_engine="scalar"):
        self.c = c
        self.banks = banks
        self.nb = len(banks)
        self.ident = ident
        self.tag = tag
        self.evac_engine = evac_engine
        self.t = 0
        self.tsem = "tr_" + tag
        self.esem = "tre_" + tag
        self.freed_at = []

    def emit(self, src_ap, dst_ap, extra_consumer=None):
        c = self.c
        bank = self.banks[self.t % self.nb]
        if self.t >= self.nb:
            c.wait("tensor", self.esem, self.freed_at[self.t - self.nb])
        c.op("tensor",
             lambda e, src_ap=src_ap, bank=bank: e.transpose(bank[:], src_ap, self.ident[:]),
             inc=(self.tsem, 1))
        c.wait(self.evac_engine, self.tsem, self.t + 1)
        if self.evac_engine == "scalar":
            c.op("scalar",
                 lambda e, dst_ap=dst_ap, bank=bank: e.copy(dst_ap, bank[:]),
                 inc=(self.esem, 1))
        else:
            c.op(self.evac_engine,
                 lambda e, dst_ap=dst_ap, bank=bank: e.tensor_copy(dst_ap, bank[:]),
                 inc=(self.esem, 1))
        if extra_consumer is not None:
            eng2, fn2 = extra_consumer
            c.wait(eng2, self.esem, c.val[self.esem])
            c.op(eng2, lambda e, bank=bank, fn2=fn2: fn2(e, bank), inc=(self.esem, 1))
        self.freed_at.append(c.val[self.esem])
        self.t += 1


def build_program(debug_outputs=(), phases=("a1", "a2", "a3", "b", "c")):
    nc = bass.Bass("TRN2", target_bir_lowering=False, detect_race_conditions=False)

    def scratch_kind(name):
        return "ExternalOutput" if name in debug_outputs else "Internal"

    x_d = nc.dram_tensor("x", [BB, S, D], F32, kind="ExternalInput").ap()
    wqkv_d = nc.dram_tensor("wqkv", [D, FQKV], F32, kind="ExternalInput").ap()
    bqkv_d = nc.dram_tensor("bqkv", [FQKV], F32, kind="ExternalInput").ap()
    wo_d = nc.dram_tensor("wo", [D, D], F32, kind="ExternalInput").ap()
    bo_d = nc.dram_tensor("bo", [D], F32, kind="ExternalInput").ap()
    rope_d = nc.dram_tensor("rope", [256, 32, 2], F32, kind="ExternalInput").ap()
    wg_d = nc.dram_tensor("wg", [I_FFN, D], F32, kind="ExternalInput").ap()
    wu_d = nc.dram_tensor("wu", [I_FFN, D], F32, kind="ExternalInput").ap()
    wd_d = nc.dram_tensor("wd", [D, I_FFN], F32, kind="ExternalInput").ap()
    out_d = nc.dram_tensor("out", [BB, S, D], F32, kind="ExternalOutput").ap()

    qt_s = nc.dram_tensor("qt_s", [BB, H, HD, S], F32, kind=scratch_kind("qt_s")).ap()
    kt_s = nc.dram_tensor("kt_s", [BB, H, HD, S], F32, kind=scratch_kind("kt_s")).ap()
    v_s = nc.dram_tensor("v_s", [BB, H, S, HD], F32, kind=scratch_kind("v_s")).ap()
    ao_s = nc.dram_tensor("ao_s", [BB, D, S], F32, kind=scratch_kind("ao_s")).ap()
    o1_s = nc.dram_tensor("o1_s", [BB, S, D], F32, kind=scratch_kind("o1_s")).ap()
    o2_s = nc.dram_tensor("o2_s", [BB, S, D], F32, kind=scratch_kind("o2_s")).ap()
    wq_dbg = nc.dram_tensor("wq_dbg", [6, 128, 11264], BF16, kind=scratch_kind("wq_dbg")).ap() \
        if "wq_dbg" in debug_outputs else None
    q_dbg = nc.dram_tensor("q_dbg", [2, 128, 1408], BF16, kind=scratch_kind("q_dbg")).ap() \
        if "q_dbg" in debug_outputs else None
    q_dbg2 = nc.dram_tensor("q_dbg2", [4, 128, 1408], F32, kind=scratch_kind("q_dbg2")).ap() \
        if "q_dbg2" in debug_outputs else None
    sw_dbg = nc.dram_tensor("sw_dbg", [4, 128, 1024], F32, kind=scratch_kind("sw_dbg")).ap() \
        if "sw_dbg" in debug_outputs else None
    ln_dbg = nc.dram_tensor("ln_dbg", [4, 128, NTILES], F32, kind=scratch_kind("ln_dbg")).ap() \
        if "ln_dbg" in debug_outputs else None
    hq_dbg = nc.dram_tensor("hq_dbg", [4, 128, 512], BF16, kind=scratch_kind("hq_dbg")).ap() \
        if "hq_dbg" in debug_outputs else None
    dp_s = nc.dram_tensor("dp_s", [BB, S, D], F32, kind=scratch_kind("dp_s")).ap()

    est = ExitStack()
    c = Ctx(nc, est)
    c.dummy = est.enter_context(nc.sbuf_tensor("barrier_dummy", [1, 12], F32))

    ident_f = est.enter_context(nc.sbuf_tensor("ident_f", [128, 128], F32))
    ident_b = est.enter_context(nc.sbuf_tensor("ident_b", [128, 128], BF16))
    mask_b = est.enter_context(nc.sbuf_tensor("mask_b", [128, 384], BF16))
    stats_m1 = est.enter_context(nc.sbuf_tensor("stats_m1", [128, NTILES], F32))
    stats_v1 = est.enter_context(nc.sbuf_tensor("stats_v1", [128, NTILES], F32))
    stats_m2 = est.enter_context(nc.sbuf_tensor("stats_m2", [128, NTILES], F32))
    stats_v2 = est.enter_context(nc.sbuf_tensor("stats_v2", [128, NTILES], F32))
    rstd1 = est.enter_context(nc.sbuf_tensor("rstd1", [128, NTILES], F32))
    rstd2 = est.enter_context(nc.sbuf_tensor("rstd2", [128, NTILES], F32))
    epsc = est.enter_context(nc.sbuf_tensor("epsc", [128, 1], F32))
    ubuf = est.enter_context(nc.sbuf_tensor("ubuf", [128, NTILES], F32))
    wbuf = est.enter_context(nc.sbuf_tensor("wbuf", [128, NTILES], F32))

    def emit_rstd(c, stats_m, stats_v, rstd_buf, fence_sem, fence_val):
        def vop(fn):
            c.op("vector", fn, inc=("v_rst", 1))
            c.wait("vector", "v_rst")
        c.wait("vector", fence_sem, fence_val)
        vop(lambda e: e.tensor_scalar(
            stats_m[:], stats_m[:], 2.0 ** -10, None, ALU.mult))
        vop(lambda e: e.tensor_tensor(wbuf[:], stats_m[:], stats_m[:], ALU.mult))
        vop(lambda e: e.tensor_scalar(
            ubuf[:], stats_v[:], 2.0 ** -10, EPS, ALU.mult, ALU.add))
        vop(lambda e: e.tensor_tensor(ubuf[:], ubuf[:], wbuf[:], ALU.subtract))
        c.op("vector", _nop, inc=("v_rst2", 1))
        c.wait("scalar", "v_rst2", c.val["v_rst2"])
        c.op("scalar", lambda e: e.activation(
            rstd_buf[:], ubuf[:], AF.Ln), inc=("s_rst", 1))
        c.wait("scalar", "s_rst")
        c.op("scalar", lambda e: e.activation(
            rstd_buf[:], rstd_buf[:], AF.Exp, scale=-0.5), inc=("s_rst", 1))
        c.wait("vector", "s_rst", c.val["s_rst"])
        vop(lambda e: e.tensor_tensor(wbuf[:], rstd_buf[:], rstd_buf[:], ALU.mult))
        vop(lambda e: e.tensor_tensor(wbuf[:], wbuf[:], ubuf[:], ALU.mult))
        vop(lambda e: e.tensor_scalar(
            wbuf[:], wbuf[:], -0.5, 1.5, ALU.mult, ALU.add))
        vop(lambda e: e.tensor_tensor(rstd_buf[:], rstd_buf[:], wbuf[:], ALU.mult))
        c.op("vector", _nop, inc=("v_rstd_done", 1))

    def phase_p0():
        c.op("gpsimd", lambda e: e.memset(ident_f[:], 1.0))
        c.op("gpsimd", lambda e: e.affine_select(
            ident_f[:], ident_f[:], [[-1, 128]], ALU.is_equal, 0.0,
            base=0, channel_multiplier=1))
        c.op("gpsimd", lambda e: e.memset(ident_b[:], 1.0))
        c.op("gpsimd", lambda e: e.affine_select(
            ident_b[:], ident_b[:], [[-1, 128]], ALU.is_equal, 0.0,
            base=0, channel_multiplier=1))
        c.op("gpsimd", lambda e: e.memset(epsc[:], EPS))
        c.op("gpsimd", lambda e: e.memset(mask_b[:], 0.0))
        c.op("gpsimd", lambda e: e.affine_select(
            mask_b[:, 0:128], mask_b[:, 0:128], [[1, 128]], ALU.is_ge, -1e30,
            base=0, channel_multiplier=-1))
        c.op("gpsimd", lambda e: e.affine_select(
            mask_b[:, 256:384], mask_b[:, 256:384], [[-1, 128]], ALU.is_ge, -1e30,
            base=0, channel_multiplier=1), inc=("g_const", 1))
        c.wait("tensor", "g_const", 1)
        c.wait("scalar", "g_const", 1)
        c.wait("vector", "g_const", 1)

    def phase_a1():
        with ExitStack() as ph:
            sb = lambda name, shape, dt=F32: ph.enter_context(nc.sbuf_tensor(name, shape, dt))
            wq_hi = sb("wq_hi", [128, 8, FQKV], BF16)
            wq_lo = sb("wq_lo", [128, 8, FQKV], BF16)
            wstage = [sb("wstage0", [128, FQKV])]
            brep = sb("brep", [128, FQKV])
            rop = sb("rop", [128, 128])
            tabs = {}
            for par in (0, 1):
                for cs in (0, 1):
                    for qk in (0, 1):
                        tabs[(par, cs, qk)] = sb(f"tab{par}{cs}{qk}", [128, 512])
            xt = [sb(f"xt{i}", [128, D]) for i in range(2)]
            xT_hi = sb("xT_hi", [128, D], BF16)
            xT_lo = sb("xT_lo", [128, D], BF16)
            xTf = sb("xTf", [128, D])
            qkv_sb = sb("qkv_sb", [128, FQKV])
            qr = sb("qr", [128, D])
            kr = sb("kr", [128, D])
            rtmp = sb("rtmp", [128, 512])
            qts = [sb(f"qts{i}", [128, D]) for i in range(2)]
            kts = [sb(f"kts{i}", [128, D]) for i in range(2)]
            vs_ = [sb(f"vs{i}", [128, D]) for i in range(2)]
            with ExitStack() as ps:
                pbanks = [ps.enter_context(nc.psum_tensor(f"ptrA1_{i}", [128, 128], F32))
                          for i in range(4)]
                pqkv = [ps.enter_context(nc.psum_tensor(f"pqkv{i}", [128, 512], F32))
                        for i in range(2)]
                pipe = TransposePipe(c, pbanks, ident_f, "a1")

                for fc in range(8):
                    buf = wstage[0]
                    if fc >= 1:
                        c.wait("sync", "v_wsplit", 2 * fc)
                    c.dma("sync", out=buf[:], in_=wqkv_d[fc * 128:(fc + 1) * 128, :], sem="ld_w")
                    c.wait("vector", "ld_w", (fc + 1) * 16)
                    c.op("vector", lambda e, fc=fc, buf=buf: e.tensor_copy(wq_hi[:, fc, :], buf[:]),
                         inc=("v_wsplit", 1))
                    c.op("vector", lambda e, fc=fc, buf=buf: e.tensor_tensor(
                        wq_lo[:, fc, :], buf[:], wq_hi[:, fc, :], ALU.subtract),
                        inc=("v_wsplit", 1))
                c.dma("sync", out=brep[:], in_=bqkv_d.unsqueeze(0).broadcast_to([128, FQKV]),
                      sem="ld_w")
                rflat = rope_d.rearrange("p j c -> p (j c)")
                c.dma("sync", out=rop[:, 0:64], in_=rflat[0:128, :], sem="ld_w")
                c.dma("sync", out=rop[:, 64:128], in_=rflat[128:256, :], sem="ld_w")
                c.wait("vector", "ld_w")
                for par in (0, 1):
                    for cs in (0, 1):
                        src = rop[:, 64 * par + cs: 64 * par + 64: 2]
                        src_b = src.unsqueeze(1).broadcast_to([128, H, 32])
                        for qk in (0, 1):
                            dst = tabs[(par, cs, qk)][:].rearrange("p (h j) -> p h j", j=32)
                            scale = 0.125 if qk == 0 else 1.0
                            c.op("vector", lambda e, dst=dst, src_b=src_b, scale=scale:
                                 e.tensor_scalar(dst, src_b, scale, None, ALU.mult),
                                 inc=("v_tab", 1))
                c.wait("tensor", "v_wsplit", 16)
                c.wait("tensor", "v_tab", 8)

                for it in range(NTILES):
                    b, ti = divmod(it, NT)
                    par = ti % 2
                    xb = xt[it % 2]
                    if it >= 2:
                        c.wait("sync", "t_xdone", it - 1)
                    c.dma("sync", out=xb[:], in_=x_d[b, ti * 128:(ti + 1) * 128, :], sem="ld_x")
                    c.wait("tensor", "ld_x", (it + 1) * 16)
                    for ch in range(8):
                        def lo_resid(e, bank, ch=ch):
                            return e.tensor_tensor(
                                xTf[:, ch * 128:(ch + 1) * 128], bank[:],
                                xT_hi[:, ch * 128:(ch + 1) * 128], ALU.subtract)
                        pipe.emit(xb[:, ch * 128:(ch + 1) * 128],
                                  xT_hi[:, ch * 128:(ch + 1) * 128],
                                  extra_consumer=("vector", lo_resid))
                        if ch == 7:
                            c.op("tensor", _nop, inc=("t_xdone", 1))
                    c.op("vector", lambda e: e.tensor_copy(xT_lo[:], xTf[:]), inc=("v_xlo", 1))

                    c.wait("tensor", "v_xlo", it + 1)
                    for fc in range(6):
                        pq = pqkv[fc % 2]
                        if it * 6 + fc >= 2:
                            c.wait("tensor", "v_qkvevac", it * 6 + fc - 1)
                        for ch in range(8):
                            c.op("tensor", lambda e, pq=pq, ch=ch, fc=fc: e.matmul(
                                pq[:], xT_hi[:, ch * 128:(ch + 1) * 128],
                                wq_hi[:, ch, fc * 512:(fc + 1) * 512],
                                start=(ch == 0), stop=False))
                            c.op("tensor", lambda e, pq=pq, ch=ch, fc=fc: e.matmul(
                                pq[:], xT_hi[:, ch * 128:(ch + 1) * 128],
                                wq_lo[:, ch, fc * 512:(fc + 1) * 512],
                                start=False, stop=False))
                            c.op("tensor", lambda e, pq=pq, ch=ch, fc=fc: e.matmul(
                                pq[:], xT_lo[:, ch * 128:(ch + 1) * 128],
                                wq_hi[:, ch, fc * 512:(fc + 1) * 512],
                                start=False, stop=(ch == 7)),
                                inc=(("t_qkv", 1) if ch == 7 else None))
                        c.wait("vector", "t_qkv", it * 6 + fc + 1)
                        if it >= 1 and fc == 0:
                            c.wait("vector", "s_vcopy", it)
                        c.op("vector", lambda e, pq=pq, fc=fc: e.tensor_tensor(
                            qkv_sb[:, fc * 512:(fc + 1) * 512], pq[:],
                            brep[:, fc * 512:(fc + 1) * 512], ALU.add),
                            inc=("v_qkvevac", 1))

                    if it >= 1:
                        c.wait("vector", "tr_a1", it * 24)
                    for qk, (dst, base) in enumerate(((qr, 0), (kr, D))):
                        x0 = qkv_sb[:, base + 0:base + D:2]
                        x1 = qkv_sb[:, base + 1:base + D:2]
                        ct = tabs[(par, 0, qk)][:]
                        st_ = tabs[(par, 1, qk)][:]
                        d0 = dst[:, 0:D:2]
                        d1 = dst[:, 1:D:2]
                        c.op("vector", lambda e, d0=d0, x0=x0, ct=ct: e.tensor_tensor(d0, x0, ct, ALU.mult))
                        c.op("vector", lambda e, x1=x1, st_=st_: e.tensor_tensor(rtmp[:], x1, st_, ALU.mult))
                        c.op("vector", lambda e, d0=d0: e.tensor_tensor(d0, d0, rtmp[:], ALU.subtract),
                             inc=("v_rope", 1))
                        c.op("vector", lambda e, d1=d1, x1=x1, ct=ct: e.tensor_tensor(d1, x1, ct, ALU.mult))
                        c.op("vector", lambda e, x0=x0, st_=st_: e.tensor_tensor(rtmp[:], x0, st_, ALU.mult))
                        c.op("vector", lambda e, d1=d1: e.tensor_tensor(d1, d1, rtmp[:], ALU.add),
                             inc=("v_rope", 1))

                    qb, kb, vb = qts[it % 2], kts[it % 2], vs_[it % 2]
                    if it >= 2:
                        c.wait("scalar", "st_qkt", (it - 1) * 48)
                    c.wait("tensor", "v_rope", (it + 1) * 4)
                    for src, dst in ((qr, qb), (kr, kb)):
                        for ch in range(8):
                            pipe.emit(src[:, ch * 128:(ch + 1) * 128],
                                      dst[:, ch * 128:(ch + 1) * 128])
                    c.wait("scalar", "v_qkvevac", (it + 1) * 6)
                    c.op("scalar", lambda e, vb=vb: e.copy(vb[:], qkv_sb[:, 2 * D:3 * D]),
                         inc=("s_vcopy", 1))
                    c.wait("sync", "tre_a1")
                    c.wait("sync", "s_vcopy", it + 1)
                    qdst = qt_s.rearrange("b h d s -> b (h d) s")[b, :, ti * 128:(ti + 1) * 128] \
                        .rearrange("(ch p) t -> p ch t", p=128)
                    kdst = kt_s.rearrange("b h d s -> b (h d) s")[b, :, ti * 128:(ti + 1) * 128] \
                        .rearrange("(ch p) t -> p ch t", p=128)
                    vdst = v_s[b, :, ti * 128:(ti + 1) * 128, :].transpose([1, 0, 2])
                    c.dma("sync", out=qdst, in_=qb[:].rearrange("p (ch t) -> p ch t", t=128), sem="st_qkt")
                    c.dma("sync", out=kdst, in_=kb[:].rearrange("p (ch t) -> p ch t", t=128), sem="st_qkt")
                    c.dma("sync", out=vdst, in_=vb[:].rearrange("p (h d) -> p h d", d=HD), sem="st_qkt")
                c.wait("sync", "st_qkt", c.val["st_qkt"])
        c.barrier("a1")

    def phase_a2():
        with ExitStack() as ph:
            sb = lambda name, shape, dt=F32: ph.enter_context(nc.sbuf_tensor(name, shape, dt))
            kt_sb = [sb(f"kt_sb{i}", [64, S]) for i in range(2)]
            qt_sb = [sb(f"qt_sb{i}", [64, S]) for i in range(2)]
            v_sb = [sb(f"v_sb{i}", [128, NT * HD]) for i in range(2)]
            junk = sb("junk", [128, 384])
            p_sb = sb("p_sb", [128, 384])
            pt_sb = sb("pt_sb", [128, 384])
            negm = [sb(f"negm{i}", [128, 1]) for i in range(2)]
            zrow = [sb(f"zrow{i}", [128, 1]) for i in range(2)]
            lnz = [sb(f"lnz{i}", [128, 1]) for i in range(2)]
            bias2 = [sb(f"bias2{i}", [128, 1]) for i in range(2)]
            aoh = [sb(f"aoh{i}", [64, S]) for i in range(2)]
            with ExitStack() as ps:
                pscore = [ps.enter_context(nc.psum_tensor(f"pscore{i}", [128, 384], F32))
                          for i in range(2)]
                ptr2 = [ps.enter_context(nc.psum_tensor(f"ptrA2_{i}", [128, 128], F32))
                        for i in range(3)]
                pao = [ps.enter_context(nc.psum_tensor(f"paoA2_{i}", [64, 128], F32))
                       for i in range(2)]
                pipe = TransposePipe(c, ptr2, ident_f, "a2", evac_engine="vector")

                def load_head(j):
                    b, h = divmod(j, H)
                    sem = f"ld_a2_{j % 2}"
                    if j >= 2:
                        c.wait("sync", "t_hd", j - 1)
                    c.dma("sync", out=kt_sb[j % 2][:], in_=kt_s[b, h], sem=sem)
                    c.dma("sync", out=qt_sb[j % 2][:], in_=qt_s[b, h], sem=sem)
                    c.dma("sync", out=v_sb[j % 2][:].rearrange("p (kt d) -> p kt d", d=HD),
                          in_=v_s[b, h].rearrange("(kt p) d -> p kt d", p=128), sem=sem)

                load_head(0)
                for j in range(BB * H):
                    b, h = divmod(j, H)
                    kt, qt, vt, ao = kt_sb[j % 2], qt_sb[j % 2], v_sb[j % 2], aoh[j % 2]
                    if j + 1 < BB * H:
                        load_head(j + 1)
                    c.wait("tensor", f"ld_a2_{j % 2}", (j // 2 + 1) * 48)
                    if j >= 2:
                        c.wait("scalar", "st_ao", (j - 1) * 16)
                    for i in range(NT):
                        g = j * NT + i
                        kt0 = max(i - 1, 0)
                        kt1 = min(i + 1, NT - 1)
                        nch = kt1 - kt0 + 1
                        kcols = nch * 128
                        psc = pscore[i % 2]
                        moff = 128 if i == 0 else 0
                        if g >= 2:
                            c.wait("tensor", "s_exp2", g - 1)
                        c.op("tensor", lambda e, psc=psc, qt=qt, i=i, kt=kt, kt0=kt0, kcols=kcols:
                             e.matmul(psc[:, 0:kcols], qt[:, i * 128:(i + 1) * 128],
                                      kt[:, kt0 * 128:kt0 * 128 + kcols],
                                      start=True, stop=False))
                        c.op("tensor", lambda e, psc=psc, kcols=kcols, moff=moff:
                             e.matmul(psc[:, 0:kcols], ident_b[:],
                                      mask_b[:, moff:moff + kcols],
                                      start=False, stop=True),
                             inc=("t_qk", 1))
                        nm, zr, lz, b2 = negm[g % 2], zrow[g % 2], lnz[g % 2], bias2[g % 2]
                        c.wait("vector", "t_qk", g + 1)
                        if g >= 1:
                            c.wait("vector", "s_exp1", g - 1)
                        c.op("vector", lambda e, psc=psc, kcols=kcols, nm=nm:
                             e.reduce_max(nm[:], psc[:, 0:kcols], AX.X, negate=True),
                             inc=("v_max", 1))
                        c.wait("scalar", "v_max", g + 1)
                        c.op("scalar", lambda e, psc=psc, kcols=kcols, nm=nm, zr=zr:
                             e.activation(junk[:, 0:kcols], psc[:, 0:kcols], AF.Exp,
                                          bias=nm[:, 0:1], accum_out=zr[:]), inc=("s_e1f", 1))
                        c.wait("scalar", "s_e1f")
                        c.op("scalar", lambda e, lz=lz, zr=zr: e.activation(lz[:], zr[:], AF.Ln),
                             inc=("s_exp1", 1))
                        c.wait("vector", "s_exp1", g + 1)
                        if g >= 1:
                            c.wait("vector", "s_exp2", g - 1)
                        c.op("vector", lambda e, b2=b2, nm=nm, lz=lz:
                             e.tensor_tensor(b2[:], nm[:], lz[:], ALU.subtract),
                             inc=("v_bias2", 1))
                        c.wait("scalar", "v_bias2", g + 1)
                        if g >= 1:
                            c.wait("scalar", "tr_a2", pipe.t)
                        c.op("scalar", lambda e, psc=psc, kcols=kcols, b2=b2:
                             e.activation(p_sb[:, 0:kcols], psc[:, 0:kcols], AF.Exp,
                                          bias=b2[:, 0:1]),
                             inc=("s_exp2", 1))
                        c.wait("tensor", "s_exp2", g + 1)
                        if g >= 1:
                            c.wait("tensor", "t_pv", g)
                        for ch in range(nch):
                            pipe.emit(p_sb[:, ch * 128:(ch + 1) * 128],
                                      pt_sb[:, ch * 128:(ch + 1) * 128])
                        pv = pao[i % 2]
                        if g >= 2:
                            c.wait("tensor", "s_ao", g - 1)
                        c.wait("tensor", "tre_a2", pipe.t)
                        for ch in range(nch):
                            ktile = kt0 + ch
                            c.op("tensor", lambda e, pv=pv, vt=vt, ktile=ktile, ch=ch, nch=nch:
                                 e.matmul(pv[:], vt[:, ktile * HD:(ktile + 1) * HD],
                                          pt_sb[:, ch * 128:(ch + 1) * 128],
                                          start=(ch == 0), stop=(ch == nch - 1)),
                                 inc=(("t_pv", 1) if ch == nch - 1 else None))
                        c.wait("scalar", "t_pv", g + 1)
                        c.op("scalar", lambda e, ao=ao, pv=pv, i=i:
                             e.copy(ao[:, i * 128:(i + 1) * 128], pv[:]),
                             inc=("s_ao", 1))
                    c.wait("sync", "s_ao", (j + 1) * NT)
                    c.dma("sync", out=ao_s[b, h * HD:(h + 1) * HD, :], in_=ao[:], sem="st_ao")
                    c.op("tensor", _nop, inc=("t_hd", 1))
                c.wait("sync", "st_ao", c.val["st_ao"])
        c.barrier("a2")

    def phase_a3():
        with ExitStack() as ph:
            sb = lambda name, shape, dt=F32: ph.enter_context(nc.sbuf_tensor(name, shape, dt))
            wo_hi = sb("wo_hi", [128, 8, D], BF16)
            wo_lo = sb("wo_lo", [128, 8, D], BF16)
            wstage3 = sb("wstage3", [128, D])
            borep = sb("borep", [128, D])
            aot = [sb(f"aot{i}", [128, D]) for i in range(2)]
            aot_hi = sb("aot_hi", [128, D], BF16)
            aot_lo = sb("aot_lo", [128, D], BF16)
            xt3 = [sb(f"xt3_{i}", [128, D]) for i in range(2)]
            o1t = [sb(f"o1t{i}", [128, D]) for i in range(2)]
            junk3 = sb("junk3", [128, D])
            junk3b = sb("junk3b", [128, D])
            acc3 = sb("acc3", [128, 2])
            nm3 = [sb(f"nm3_{i}", [128, 1]) for i in range(2)]
            with ExitStack() as ps:
                pwo = [ps.enter_context(nc.psum_tensor(f"pwo{i}", [128, 512], F32))
                       for i in range(2)]
                for fc in range(8):
                    if fc >= 1:
                        c.wait("sync", "v_wsplit3", 2 * fc)
                    c.dma("sync", out=wstage3[:], in_=wo_d[fc * 128:(fc + 1) * 128, :], sem="ld_w3")
                    c.wait("vector", "ld_w3", (fc + 1) * 16)
                    c.op("vector", lambda e, fc=fc: e.tensor_copy(wo_hi[:, fc, :], wstage3[:]),
                         inc=("v_wsplit3", 1))
                    c.op("vector", lambda e, fc=fc: e.tensor_tensor(
                        wo_lo[:, fc, :], wstage3[:], wo_hi[:, fc, :], ALU.subtract),
                        inc=("v_wsplit3", 1))
                c.dma("sync", out=borep[:], in_=bo_d.unsqueeze(0).broadcast_to([128, D]),
                      sem="ld_w3")
                c.wait("tensor", "v_wsplit3", 16)
                c.wait("vector", "ld_w3")

                for it in range(NTILES):
                    b, ti = divmod(it, NT)
                    ab, xb, ob = aot[it % 2], xt3[it % 2], o1t[it % 2]
                    if it >= 2:
                        c.wait("sync", "t_wo", (it - 1) * 2)
                    asrc = ao_s[b, :, ti * 128:(ti + 1) * 128].rearrange(
                        "(ch p) t -> p ch t", p=128)
                    c.dma("sync", out=ab[:].rearrange("p (ch t) -> p ch t", t=128),
                          in_=asrc, sem=f"ld_ao{it % 2}")
                    c.dma("sync", out=xb[:], in_=x_d[b, ti * 128:(ti + 1) * 128, :],
                          sem=f"ld_ao{it % 2}")
                    c.wait("scalar", f"ld_ao{it % 2}", (it // 2 + 1) * 32)
                    if it >= 1:
                        c.wait("scalar", "t_wo", it * 2)
                    c.op("scalar", lambda e, ab=ab: e.copy(aot_hi[:], ab[:]), inc=("s_ahi", 1))
                    c.wait("vector", "s_ahi", it + 1)
                    c.op("vector", lambda e, ab=ab: e.tensor_tensor(
                        aot_lo[:], ab[:], aot_hi[:], ALU.subtract), inc=("v_alo", 1))
                    c.wait("tensor", "v_alo", it + 1)
                    for fc in range(2):
                        pw = pwo[fc]
                        if it >= 1:
                            c.wait("tensor", "v_oevac", it * 2 + fc - 1)
                        for ch in range(8):
                            c.op("tensor", lambda e, pw=pw, ch=ch, fc=fc: e.matmul(
                                pw[:], aot_hi[:, ch * 128:(ch + 1) * 128],
                                wo_hi[:, ch, fc * 512:(fc + 1) * 512],
                                start=(ch == 0), stop=False))
                            c.op("tensor", lambda e, pw=pw, ch=ch, fc=fc: e.matmul(
                                pw[:], aot_hi[:, ch * 128:(ch + 1) * 128],
                                wo_lo[:, ch, fc * 512:(fc + 1) * 512],
                                start=False, stop=False))
                            c.op("tensor", lambda e, pw=pw, ch=ch, fc=fc: e.matmul(
                                pw[:], aot_lo[:, ch * 128:(ch + 1) * 128],
                                wo_hi[:, ch, fc * 512:(fc + 1) * 512],
                                start=False, stop=(ch == 7)),
                                inc=(("t_wo", 1) if ch == 7 else None))
                        c.wait("vector", "t_wo", it * 2 + fc + 1)
                        if it >= 2 and fc == 0:
                            c.wait("vector", "st_o1", (it - 1) * 16)
                        c.op("vector", lambda e, pw=pw, fc=fc: e.tensor_tensor(
                            junk3[:, fc * 512:(fc + 1) * 512], pw[:],
                            borep[:, fc * 512:(fc + 1) * 512], ALU.add))
                        c.op("vector", lambda e, ob=ob, xb=xb, fc=fc: e.tensor_tensor(
                            ob[:, fc * 512:(fc + 1) * 512],
                            junk3[:, fc * 512:(fc + 1) * 512],
                            xb[:, fc * 512:(fc + 1) * 512], ALU.add),
                            inc=("v_oevac", 1))
                    c.op("vector", lambda e, ob=ob, it=it: e.tensor_reduce(
                        stats_m1[:, it:it + 1].unsqueeze(2),
                        ob[:].unsqueeze(1), AX.X, ALU.add), inc=("v_red", 1))
                    c.op("vector", lambda e, ob=ob: e.tensor_tensor(
                        junk3b[:], ob[:], ob[:], ALU.mult))
                    c.op("vector", lambda e, it=it: e.tensor_reduce(
                        stats_v1[:, it:it + 1].unsqueeze(2),
                        junk3b[:].unsqueeze(1), AX.X, ALU.add), inc=("v_red", 1))
                    c.wait("sync", "v_oevac", (it + 1) * 2)
                    c.dma("sync", out=o1_s[b, ti * 128:(ti + 1) * 128, :], in_=ob[:],
                          sem="st_o1")
                emit_rstd(c, stats_m1, stats_v1, rstd1, "v_red", None)
                c.op("vector", _nop, inc=("v_m1", 1))
                if "ln_dbg" in debug_outputs:
                    c.wait("sync", "v_rstd_done", 1)
                    c.wait("sync", "v_m1", 1)
                    c.dma("sync", out=ln_dbg[0, :, :], in_=stats_m1[:], sem="st_dbg")
                    c.dma("sync", out=ln_dbg[1, :, :], in_=stats_v1[:], sem="st_dbg")
                    c.dma("sync", out=ln_dbg[2, :, :], in_=ubuf[:], sem="st_dbg")
                    c.dma("sync", out=ln_dbg[3, :, :], in_=rstd1[:], sem="st_dbg")
                    c.wait("sync", "st_dbg", c.val["st_dbg"])
                c.wait("sync", "st_o1", c.val["st_o1"])
        c.barrier("a3")

    def phase_b():
        QN = 1408
        with ExitStack() as ph:
            sb = lambda name, shape, dt=F32: ph.enter_context(nc.sbuf_tensor(name, shape, dt))
            wgT_hi = sb("wgT_hi", [128, 8 * I_HALF], BF16)
            wgT_lo = sb("wgT_lo", [128, 8 * I_HALF], BF16)
            wuT_hi = sb("wuT_hi", [128, 8 * I_HALF], BF16)
            wuT_lo = sb("wuT_lo", [128, 8 * I_HALF], BF16)
            wdT_hi = sb("wdT_hi", [128, 11 * D], BF16)
            wdT_lo = sb("wdT_lo", [128, 11 * D], BF16)
            qa = sb("qa", [128, QN])
            qsgn = sb("qsgn", [128, QN], BF16)
            qssg = sb("qssg", [128, QN])
            qcA = sb("qcA", [128, QN], BF16)
            qcB = sb("qcB", [128, QN], BF16)
            qmr = sb("qmr", [128, 128])
            qs_ = sb("qs_", [128, 128])
            qsx = sb("qsx", [128, QN])
            qh_ = sb("qh_", [128, QN], BF16)
            ql_ = sb("ql_", [128, QN], BF16)
            wstg = [sb("wstgB0", [128, QN])]
            o1b = [sb(f"o1b{i}", [128, D]) for i in range(2)]
            xqT_hi = sb("xqT_hi", [128, D], BF16)
            xqT_lo = sb("xqT_lo", [128, D], BF16)
            hbuf = sb("hbuf", [128, 512])
            gact = sb("gact", [128, 512])
            hqT_hi = sb("hqT_hi", [128, 512], BF16)
            hqT_lo = sb("hqT_lo", [128, 512], BF16)
            dpt = [sb(f"dpt{i}", [128, D]) for i in range(2)]
            o2t = [sb(f"o2t{i}", [128, D]) for i in range(2)]
            accb = sb("accb", [128, 2])
            nm2 = [sb(f"nm2_{i}", [128, 1]) for i in range(2)]

            def emit_quant(src_ap, n):
                nb = n // 16
                a3d = qa[:, 0:n].rearrange("p (nb w) -> p nb w", w=16)
                sb3 = qs_[:, 0:nb].unsqueeze(2).broadcast_to([128, nb, 16])
                sx3 = qsx[:, 0:n].rearrange("p (nb w) -> p nb w", w=16)
                c.wait("scalar", "q_v")
                c.op("scalar", lambda e, src_ap=src_ap, n=n: e.activation(
                    qa[:, 0:n], src_ap, AF.Abs), inc=("q_s", 1))
                c.op("scalar", lambda e, src_ap=src_ap, n=n: e.activation(
                    qsgn[:, 0:n], src_ap, AF.Sign), inc=("q_s", 1))
                c.wait("vector", "q_s")
                c.op("vector", lambda e, a3d=a3d, nb=nb: e.tensor_reduce(
                    qmr[:, 0:nb].unsqueeze(2), a3d, AX.X, ALU.max), inc=("q_v", 1))
                c.wait("vector", "q_v")
                c.op("vector", lambda e, nb=nb: e.tensor_scalar(
                    qs_[:, 0:nb], qmr[:, 0:nb], 1.0 / 6.0, 1e-12, ALU.mult, ALU.max),
                    inc=("q_v", 1))
                c.wait("vector", "q_v")
                c.op("vector", lambda e, sx3=sx3, sb3=sb3: e.tensor_scalar(
                    sx3, sb3, 1.0, None, ALU.mult), inc=("q_v", 1))
                c.wait("vector", "q_v")
                c.op("vector", lambda e, n=n: e.scalar_tensor_tensor(
                    qcA[:, 0:n], qsx[:, 0:n], MIDS[0], qa[:, 0:n], ALU.mult, ALU.is_lt))
                for j in (1, 2, 3):
                    c.op("vector", lambda e, n=n, j=j: e.scalar_tensor_tensor(
                        qcB[:, 0:n], qsx[:, 0:n], MIDS[j], qa[:, 0:n], ALU.mult, ALU.is_lt))
                    c.op("vector", lambda e, n=n: e.tensor_tensor(
                        qcA[:, 0:n], qcA[:, 0:n], qcB[:, 0:n], ALU.add))
                c.op("vector", lambda e, n=n: e.scalar_tensor_tensor(
                    qssg[:, 0:n], qsx[:, 0:n], MIDS[4], qa[:, 0:n], ALU.mult, ALU.is_lt))
                c.op("vector", lambda e, n=n: e.scalar_tensor_tensor(
                    qcB[:, 0:n], qsx[:, 0:n], MIDS[5], qa[:, 0:n], ALU.mult, ALU.is_lt))
                c.op("vector", lambda e, n=n: e.tensor_tensor(
                    qssg[:, 0:n], qssg[:, 0:n], qcB[:, 0:n], ALU.add))
                c.op("vector", lambda e, n=n: e.scalar_tensor_tensor(
                    qcA[:, 0:n], qcA[:, 0:n], 0.5, qssg[:, 0:n], ALU.mult, ALU.add))
                c.op("vector", lambda e, n=n: e.scalar_tensor_tensor(
                    qcB[:, 0:n], qsx[:, 0:n], MIDS[6], qa[:, 0:n], ALU.mult, ALU.is_lt))
                c.op("vector", lambda e, n=n: e.scalar_tensor_tensor(
                    qcA[:, 0:n], qcB[:, 0:n], 2.0, qcA[:, 0:n], ALU.mult, ALU.add),
                    inc=("q_v", 1))
                c.wait("vector", "q_v")
                c.op("vector", lambda e, n=n: e.tensor_tensor(
                    qssg[:, 0:n], qsgn[:, 0:n], qsx[:, 0:n], ALU.mult))
                c.op("vector", lambda e, n=n: e.tensor_tensor(
                    qssg[:, 0:n], qcA[:, 0:n], qssg[:, 0:n], ALU.mult))
                c.op("vector", lambda e, n=n: e.tensor_copy(qh_[:, 0:n], qssg[:, 0:n]),
                     inc=("q_v", 1))
                c.op("vector", lambda e, n=n: e.tensor_tensor(
                    ql_[:, 0:n], qssg[:, 0:n], qh_[:, 0:n], ALU.subtract), inc=("q_v", 1))
                c.wait("vector", "q_v")

            with ExitStack() as ps:
                pd = [ps.enter_context(nc.psum_tensor(f"pdn{i}", [128, D], F32))
                      for i in range(2)]
                pg = ps.enter_context(nc.psum_tensor("pgate", [128, 512], F32))
                pu = ps.enter_context(nc.psum_tensor("pup", [128, 512], F32))
                ptrB = [ps.enter_context(nc.psum_tensor(f"ptrB{i}", [128, 128], BF16))
                        for i in range(2)]
                pipe = TransposePipe(c, ptrB, ident_b, "b")

                W_SUB = [512] if "onesub" in debug_outputs else [512, 512, 384]
                x1_qs = {}
                for half in range(2):
                    if half == 1:
                        c.barrier("b_half")
                    wt_jobs = []
                    for rt in range(11):
                        wt_jobs.append(("g", rt, wg_d[half * I_HALF + rt * 128:
                                                     half * I_HALF + (rt + 1) * 128, :], D))
                    for rt in range(11):
                        wt_jobs.append(("u", rt, wu_d[half * I_HALF + rt * 128:
                                                     half * I_HALF + (rt + 1) * 128, :], D))
                    for rt in range(8):
                        wt_jobs.append(("d", rt, wd_d[rt * 128:(rt + 1) * 128,
                                                      half * I_HALF:(half + 1) * I_HALF], I_HALF))
                    qs_after = []
                    for jn, (kind, rt, src, n) in enumerate(wt_jobs):
                        buf = wstg[0]
                        if jn >= 1:
                            c.wait("sync", "q_s", qs_after[jn - 1])
                        c.dma("sync", out=buf[:, 0:n], in_=src, sem="ld_wb")
                        c.wait("scalar", "ld_wb", c.val["ld_wb"])
                        emit_quant(buf[:, 0:n], n)
                        qs_after.append(c.val["q_s"])
                        c.wait("tensor", "q_v")
                        if q_dbg2 is not None and half == 0 and jn == 0:
                            c.wait("sync", "q_v")
                            c.dma("sync", out=q_dbg2[0, :, 0:128], in_=qmr[:], sem="st_dbg")
                            c.dma("sync", out=q_dbg2[1, :, 0:128], in_=qs_[:], sem="st_dbg")
                            c.dma("sync", out=q_dbg2[2, :, 0:128], in_=qr_[:], sem="st_dbg")
                            c.dma("sync", out=q_dbg2[3, :, 0:n], in_=qa[:, 0:n], sem="st_dbg")
                            c.wait("sync", "st_dbg", c.val["st_dbg"])
                            c.barrier("dbgb")
                        if q_dbg is not None and half == 0 and jn == 0:
                            c.wait("sync", "q_v")
                            c.dma("sync", out=q_dbg[0, :, 0:n], in_=qh_[:, 0:n], sem="st_dbg")
                            c.dma("sync", out=q_dbg[1, :, 0:n], in_=ql_[:, 0:n], sem="st_dbg")
                            c.wait("sync", "st_dbg", 32)
                        nch = n // 128
                        for ch in range(nch):
                            for hl, (qsrc, wdst) in enumerate((
                                    (qh_, {"g": wgT_hi, "u": wuT_hi, "d": wdT_hi}[kind]),
                                    (ql_, {"g": wgT_lo, "u": wuT_lo, "d": wdT_lo}[kind]))):
                                if kind == "d":
                                    dst = wdst[:, ch * D + rt * 128: ch * D + (rt + 1) * 128]
                                else:
                                    dst = wdst[:, ch * I_HALF + rt * 128: ch * I_HALF + (rt + 1) * 128]
                                pipe.emit(qsrc[:, ch * 128:(ch + 1) * 128], dst)
                        c.wait("vector", "tr_b", pipe.t)
                    c.wait("tensor", "tre_b")
                    if wq_dbg is not None and half == 0:
                        c.wait("sync", "tre_b")
                        for di, wt in enumerate((wgT_hi, wgT_lo, wuT_hi, wuT_lo,
                                                 wdT_hi, wdT_lo)):
                            c.dma("sync", out=wq_dbg[di], in_=wt[:], sem="st_dbg")
                        c.wait("sync", "st_dbg", 96)

                    for it in range(NTILES):
                        b, ti = divmod(it, NT)
                        ob = o1b[it % 2]
                        xb = ob
                        pdn = pd[it % 2]
                        gidx = half * NTILES + it
                        if gidx >= 2:
                            c.wait("sync", "q_s", x1_qs.get(gidx - 2, 0))
                            if half == 1 and it >= 2:
                                c.wait("sync", "v_o2", (it - 1) * 2)
                        c.dma("sync", out=ob[:], in_=o1_s[b, ti * 128:(ti + 1) * 128, :],
                              sem=f"ld_o1{it % 2}")
                        if half == 1:
                            db = dpt[it % 2]
                            if it >= 2:
                                c.wait("sync", "v_o2", (it - 1) * 2)
                            c.dma("sync", out=db[:], in_=dp_s[b, ti * 128:(ti + 1) * 128, :],
                                  sem=f"ld_dp{it % 2}")
                        if gidx == 0:
                            c.wait("vector", "v_rstd_done", 1)
                            c.wait("vector", "v_m1", 1)
                        c.wait("vector", f"ld_o1{it % 2}", ((half * NTILES + it) // 2 + 1) * 16)
                        c.op("vector", lambda e, ob=ob, it=it: e.tensor_scalar(
                            ob[:], ob[:], stats_m1[:, it:it + 1], rstd1[:, it:it + 1],
                            ALU.subtract, ALU.mult), inc=("v_x1", 1))
                        c.wait("scalar", "v_x1", gidx + 1)
                        emit_quant(xb[:], D)
                        x1_qs[gidx] = c.val["q_s"]
                        c.wait("tensor", "q_v")
                        for ch in range(8):
                            pipe.emit(qh_[:, ch * 128:(ch + 1) * 128],
                                      xqT_hi[:, ch * 128:(ch + 1) * 128])
                            pipe.emit(ql_[:, ch * 128:(ch + 1) * 128],
                                      xqT_lo[:, ch * 128:(ch + 1) * 128])
                        c.wait("vector", "tr_b", pipe.t)
                        if gidx >= 2:
                            if half == 0:
                                c.wait("tensor", "s_dpe", (gidx - 1) * 2)
                            elif it >= 2:
                                c.wait("tensor", "v_o2", (gidx - NTILES - 1) * 2)
                        icol = 0
                        for sub, wsub in enumerate(W_SUB):
                            c.wait("tensor", "tre_b")
                            nsub = len(W_SUB)
                            if gidx * nsub + sub >= 1:
                                c.wait("tensor", "s_silu", gidx * nsub + sub)
                                c.wait("tensor", "v_h", gidx * nsub + sub)
                            for ch in range(8):
                                co = ch * I_HALF + icol
                                c.op("tensor", lambda e, ch=ch, co=co, wsub=wsub: e.matmul(
                                    pg[:, 0:wsub], xqT_hi[:, ch * 128:(ch + 1) * 128],
                                    wgT_hi[:, co:co + wsub], start=(ch == 0), stop=False))
                                c.op("tensor", lambda e, ch=ch, co=co, wsub=wsub: e.matmul(
                                    pg[:, 0:wsub], xqT_hi[:, ch * 128:(ch + 1) * 128],
                                    wgT_lo[:, co:co + wsub], start=False, stop=False))
                                c.op("tensor", lambda e, ch=ch, co=co, wsub=wsub: e.matmul(
                                    pg[:, 0:wsub], xqT_lo[:, ch * 128:(ch + 1) * 128],
                                    wgT_hi[:, co:co + wsub], start=False, stop=(ch == 7)),
                                    inc=(("t_g", 1) if ch == 7 else None))
                            for ch in range(8):
                                co = ch * I_HALF + icol
                                c.op("tensor", lambda e, ch=ch, co=co, wsub=wsub: e.matmul(
                                    pu[:, 0:wsub], xqT_hi[:, ch * 128:(ch + 1) * 128],
                                    wuT_hi[:, co:co + wsub], start=(ch == 0), stop=False))
                                c.op("tensor", lambda e, ch=ch, co=co, wsub=wsub: e.matmul(
                                    pu[:, 0:wsub], xqT_hi[:, ch * 128:(ch + 1) * 128],
                                    wuT_lo[:, co:co + wsub], start=False, stop=False))
                                c.op("tensor", lambda e, ch=ch, co=co, wsub=wsub: e.matmul(
                                    pu[:, 0:wsub], xqT_lo[:, ch * 128:(ch + 1) * 128],
                                    wuT_hi[:, co:co + wsub], start=False, stop=(ch == 7)),
                                    inc=(("t_u", 1) if ch == 7 else None))
                            if sw_dbg is not None and gidx == DBG_GIDX and sub == 0:
                                c.wait("sync", "v_x1", 1)
                                c.dma("sync", out=sw_dbg[0], in_=xb[:], sem="st_dbg")
                                c.wait("vector", "t_g", 1)
                                c.op("vector", lambda e: e.tensor_copy(qssg[:, 0:512], pg[:, 0:512]),
                                     inc=("v_dbg", 1))
                                c.wait("sync", "v_dbg", 1)
                                c.dma("sync", out=sw_dbg[1, :, 0:512], in_=qssg[:, 0:512], sem="st_dbg")
                                c.wait("vector", "t_u", 1)
                                c.op("vector", lambda e: e.tensor_copy(qssg[:, 512:1024], pu[:, 0:512]),
                                     inc=("v_dbg", 1))
                                c.wait("sync", "v_dbg", 2)
                                c.dma("sync", out=sw_dbg[2, :, 0:512], in_=qssg[:, 512:1024], sem="st_dbg")
                                c.wait("sync", "st_dbg", c.val["st_dbg"])
                                c.barrier("dbg2")
                            c.wait("scalar", "t_g", gidx * nsub + sub + 1)
                            if sw_dbg is not None and gidx == DBG_GIDX and sub == 0:
                                dbg_h = True
                            else:
                                dbg_h = False
                            c.wait("scalar", "v_h")
                            c.op("scalar", lambda e, wsub=wsub: e.activation(
                                gact[:, 0:wsub], pg[:, 0:wsub], AF.Silu), inc=("s_silu", 1))
                            c.wait("vector", "t_u", gidx * nsub + sub + 1)
                            c.wait("vector", "s_silu", gidx * nsub + sub + 1)
                            c.wait("vector", "q_s")
                            c.op("vector", lambda e, wsub=wsub: e.tensor_tensor(
                                hbuf[:, 0:wsub], gact[:, 0:wsub], pu[:, 0:wsub], ALU.mult),
                                inc=("v_h", 1))
                            if dbg_h:
                                c.wait("sync", "v_h", 1)
                                c.dma("sync", out=sw_dbg[3, :, 0:wsub], in_=hbuf[:, 0:wsub],
                                      sem="st_dbg")
                                c.wait("sync", "st_dbg", c.val["st_dbg"])
                                c.barrier("dbg3")
                            c.wait("scalar", "v_h")
                            emit_quant(hbuf[:, 0:wsub], wsub)
                            if dbg_h and q_dbg2 is not None:
                                c.wait("sync", "q_v")
                                c.dma("sync", out=q_dbg2[0, :, 0:128], in_=qmr[:], sem="st_dbg")
                                c.dma("sync", out=q_dbg2[1, :, 0:128], in_=qs_[:], sem="st_dbg")
                                c.dma("sync", out=q_dbg2[2, :, 0:128], in_=qr_[:], sem="st_dbg")
                                c.dma("sync", out=q_dbg2[3, :, 0:wsub], in_=qa[:, 0:wsub], sem="st_dbg")
                                c.wait("sync", "st_dbg", c.val["st_dbg"])
                                c.barrier("dbg5")
                            nci = wsub // 128
                            c.wait("tensor", "q_v")
                            for ci in range(nci):
                                pipe.emit(qh_[:, ci * 128:(ci + 1) * 128],
                                          hqT_hi[:, ci * 128:(ci + 1) * 128])
                                pipe.emit(ql_[:, ci * 128:(ci + 1) * 128],
                                          hqT_lo[:, ci * 128:(ci + 1) * 128])
                            if hq_dbg is not None and gidx == DBG_GIDX and sub == 0:
                                c.wait("sync", "q_v")
                                c.dma("sync", out=hq_dbg[0, :, 0:wsub], in_=qh_[:, 0:wsub], sem="st_dbg")
                                c.dma("sync", out=hq_dbg[1, :, 0:wsub], in_=ql_[:, 0:wsub], sem="st_dbg")
                                c.wait("sync", "tre_b")
                                c.dma("sync", out=hq_dbg[2, :, 0:wsub], in_=hqT_hi[:, 0:wsub], sem="st_dbg")
                                c.dma("sync", out=hq_dbg[3, :, 0:wsub], in_=hqT_lo[:, 0:wsub], sem="st_dbg")
                                c.wait("sync", "st_dbg", c.val["st_dbg"])
                                c.barrier("dbg4")
                            c.wait("vector", "tr_b", pipe.t)
                            c.wait("tensor", "tre_b")
                            first = (sub == 0)
                            last = (sub == len(W_SUB) - 1)
                            for ci in range(nci):
                                cig = (icol // 128) + ci
                                for hc in range(2):
                                    for ps_, (lhs, rhs) in enumerate((
                                            (hqT_hi, wdT_hi), (hqT_hi, wdT_lo), (hqT_lo, wdT_hi))):
                                        st = first and ci == 0 and ps_ == 0
                                        sp = last and ci == nci - 1 and ps_ == 2
                                        c.op("tensor", lambda e, pdn=pdn, lhs=lhs, rhs=rhs,
                                             ci=ci, cig=cig, hc=hc, st=st, sp=sp: e.matmul(
                                                 pdn[:, hc * 512:(hc + 1) * 512],
                                                 lhs[:, ci * 128:(ci + 1) * 128],
                                                 rhs[:, cig * D + hc * 512:cig * D + (hc + 1) * 512],
                                                 start=st, stop=sp),
                                             inc=(("t_dn", 1) if (sp and hc == 1) else None))
                            icol += wsub
                        if half == 0:
                            dpo = dpt[it % 2]
                            if it >= 2:
                                c.wait("scalar", "st_dp", (it - 1) * 16)
                            c.wait("scalar", "t_dn", gidx + 1)
                            c.op("scalar", lambda e, dpo=dpo, pdn=pdn: e.copy(
                                dpo[:, 0:512], pdn[:, 0:512]), inc=("s_dpe", 1))
                            c.op("scalar", lambda e, dpo=dpo, pdn=pdn: e.copy(
                                dpo[:, 512:D], pdn[:, 512:D]), inc=("s_dpe", 1))
                            c.wait("sync", "s_dpe", (it + 1) * 2)
                            c.dma("sync", out=dp_s[b, ti * 128:(ti + 1) * 128, :],
                                  in_=dpo[:], sem="st_dp")
                        else:
                            o2b = o2t[it % 2]
                            nm = nm2[it % 2]
                            c.wait("vector", "t_dn", gidx + 1)
                            c.wait("vector", f"ld_dp{it % 2}", (it // 2 + 1) * 16)
                            if it >= 2:
                                c.wait("vector", "st_o2", (it - 1) * 16)
                                c.wait("vector", "s_sq2", it - 1)
                            dbl = dpt[it % 2]
                            for hc in range(2):
                                c.op("vector", lambda e, pdn=pdn, dbl=dbl, o2b=o2b, hc=hc:
                                     e.tensor_tensor(
                                         o2b[:, hc * 512:(hc + 1) * 512],
                                         pdn[:, hc * 512:(hc + 1) * 512],
                                         dbl[:, hc * 512:(hc + 1) * 512], ALU.add))
                                c.op("vector", lambda e, o2b=o2b, xb=xb, hc=hc: e.tensor_tensor(
                                    o2b[:, hc * 512:(hc + 1) * 512],
                                    o2b[:, hc * 512:(hc + 1) * 512],
                                    xb[:, hc * 512:(hc + 1) * 512], ALU.add),
                                    inc=("v_o2", 1))
                            c.op("vector", lambda e, o2b=o2b, it=it: e.tensor_reduce(
                                stats_m2[:, it:it + 1].unsqueeze(2),
                                o2b[:].unsqueeze(1), AX.X, ALU.add), inc=("v_red", 1))
                            c.op("vector", lambda e, o2b=o2b: e.tensor_tensor(
                                qsx[:, 0:D], o2b[:], o2b[:], ALU.mult))
                            c.op("vector", lambda e, it=it: e.tensor_reduce(
                                stats_v2[:, it:it + 1].unsqueeze(2),
                                qsx[:, 0:D].unsqueeze(1), AX.X, ALU.add), inc=("v_red", 1))
                            c.op("vector", _nop, inc=("s_sq2", 1))
                            c.wait("sync", "v_o2", c.val["v_o2"])
                            c.dma("sync", out=o2_s[b, ti * 128:(ti + 1) * 128, :],
                                  in_=o2b[:], sem="st_o2")
                c.wait("sync", "st_o2", c.val["st_o2"])
        c.barrier("b")

    def phase_c():
        with ExitStack() as ph:
            sb = lambda name, shape, dt=F32: ph.enter_context(nc.sbuf_tensor(name, shape, dt))
            o2c = [sb(f"o2c{i}", [128, D]) for i in range(2)]
            oc = [sb(f"oc{i}", [128, D]) for i in range(2)]
            emit_rstd(c, stats_m2, stats_v2, rstd2, "v_red", None)
            c.op("vector", _nop, inc=("v_m2", 1))
            c.wait("vector", "v_rstd_done", 2)
            for it in range(NTILES):
                b, ti = divmod(it, NT)
                ib, ub = o2c[it % 2], oc[it % 2]
                if it >= 2:
                    c.wait("sync", "v_out", it - 1)
                c.dma("sync", out=ib[:], in_=o2_s[b, ti * 128:(ti + 1) * 128, :],
                      sem=f"ld_oc{it % 2}")
                c.wait("vector", f"ld_oc{it % 2}", (it // 2 + 1) * 16)
                if it >= 2:
                    c.wait("vector", "st_out", (it - 1) * 16)
                c.op("vector", lambda e, ib=ib, ub=ub, it=it: e.tensor_scalar(
                    ub[:], ib[:], stats_m2[:, it:it + 1], rstd2[:, it:it + 1],
                    ALU.subtract, ALU.mult), inc=("v_out", 1))
                c.wait("sync", "v_out", it + 1)
                c.dma("sync", out=out_d[b, ti * 128:(ti + 1) * 128, :], in_=ub[:],
                      sem="st_out")
            c.wait("sync", "st_out", c.val["st_out"])

    phase_p0()
    if "a1" in phases:
        phase_a1()
    if "a2" in phases:
        phase_a2()
    if "a3" in phases:
        phase_a3()
    if "b" in phases:
        phase_b()
    if "c" in phases:
        phase_c()
    c.replay()
    est.close()
    return nc


_NC_CACHE = {}


def _get_nc():
    if "nc" not in _NC_CACHE:
        _NC_CACHE["nc"] = build_program()
    return _NC_CACHE["nc"]


def kernel(L0_attn_hidden_states, L0_attn_Wqkv_kernel, L0_attn_Wqkv_bias,
           L0_attn_Wo_kernel, L0_attn_Wo_bias, L0_attn_rope_cache,
           L0_ffn_w_gate, L0_ffn_w_up, L0_ffn_w_down):
    x = np.ascontiguousarray(np.asarray(L0_attn_hidden_states, dtype=np.float32))
    common = {
        "wqkv": np.ascontiguousarray(np.asarray(L0_attn_Wqkv_kernel, np.float32)),
        "bqkv": np.ascontiguousarray(np.asarray(L0_attn_Wqkv_bias, np.float32)),
        "wo": np.ascontiguousarray(np.asarray(L0_attn_Wo_kernel, np.float32)),
        "bo": np.ascontiguousarray(np.asarray(L0_attn_Wo_bias, np.float32)),
        "rope": np.ascontiguousarray(np.asarray(L0_attn_rope_cache, np.float32)),
        "wg": np.ascontiguousarray(np.asarray(L0_ffn_w_gate, np.float32)),
        "wu": np.ascontiguousarray(np.asarray(L0_ffn_w_up, np.float32)),
        "wd": np.ascontiguousarray(np.asarray(L0_ffn_w_down, np.float32)),
    }
    nc = _get_nc()
    in_maps = [dict(common, x=x[i * BB:(i + 1) * BB]) for i in range(NCORES)]
    res = run_bass_kernel_spmd(nc, in_maps, core_ids=list(range(NCORES)))
    return np.concatenate([r["out"] for r in res.results], axis=0)


# version 06
# speedup vs baseline: 1.0869x; 1.0173x over previous
import numpy as np
from contextlib import ExitStack

import concourse.bass as bass
import concourse.mybir as mybir
from concourse.bass_utils import run_bass_kernel_spmd

F32 = mybir.dt.float32
BF16 = mybir.dt.bfloat16
ALU = mybir.AluOpType
AF = mybir.ActivationFunctionType
AX = mybir.AxisListType

NCORES = 8
B_FULL = 16
BB = B_FULL // NCORES
S = 2048
D = 1024
H = 16
HD = 64
FQKV = 3 * D
NT = S // 128
NTILES = BB * NT
I_FFN = 2816
I_HALF = I_FFN // 2
EPS = 1e-5

MIDS = [0.25, 0.75, 1.25, 1.75, 2.5, 3.5, 5.0]
W2 = [0.5, 0.5, 0.5, 0.5, 1.0, 1.0, 2.0]

ENGINES = ("sync", "vector", "scalar", "tensor", "gpsimd")
DBG_GIDX = 0


def _nop(eng):
    return eng.nop()


class Ctx:

    def __init__(self, nc, est):
        self.nc = nc
        self.est = est
        self.q = {e: [] for e in ENGINES}
        self.sems = {}
        self.val = {}

    def sem(self, name):
        if name not in self.sems:
            self.sems[name] = self.est.enter_context(self.nc.semaphore(name))
            self.val[name] = 0
        return self.sems[name]

    def op(self, engine, fn, inc=None):
        if inc is None:
            self.q[engine].append(fn)
        else:
            name, n = inc
            s = self.sem(name)
            self.val[name] += n
            self.q[engine].append(lambda eng, fn=fn, s=s, n=n: fn(eng).then_inc(s, n))

    def wait(self, engine, name, value=None):
        s = self.sem(name)
        v = self.val[name] if value is None else value
        if v <= 0:
            return
        self.q[engine].append(lambda eng, s=s, v=v: eng.wait_ge(s, v))

    def dma(self, engine, out, in_, sem="dma", **kw):
        s = self.sem(sem)
        self.val[sem] += 16
        self.q[engine].append(
            lambda eng, out=out, in_=in_, s=s, kw=kw:
            eng.dma_start(out=out, in_=in_, **kw).then_inc(s, 16))

    def barrier(self, name):
        sn = name + "_bar"
        self.sem(sn)
        target = self.val[sn] + len(ENGINES)
        dum = self.dummy
        self.op("sync", _nop, inc=(sn, 1))
        self.op("tensor", _nop, inc=(sn, 1))
        self.op("vector", lambda e: e.memset(dum[0:1, 0:4], 0.0), inc=(sn, 1))
        self.op("scalar", lambda e: e.copy(dum[0:1, 4:8], dum[0:1, 0:4]), inc=(sn, 1))
        self.op("gpsimd", lambda e: e.memset(dum[0:1, 8:12], 0.0), inc=(sn, 1))
        assert self.val[sn] == target
        for e in ENGINES:
            self.wait(e, sn, target)

    def replay(self):
        nc = self.nc
        with nc.Block() as block:
            @block.sync
            def _(eng):
                for f in self.q["sync"]:
                    f(eng)

            @block.vector
            def _(eng):
                for f in self.q["vector"]:
                    f(eng)

            @block.scalar
            def _(eng):
                for f in self.q["scalar"]:
                    f(eng)

            @block.tensor
            def _(eng):
                for f in self.q["tensor"]:
                    f(eng)

            @block.gpsimd
            def _(eng):
                for f in self.q["gpsimd"]:
                    f(eng)


class TransposePipe:

    def __init__(self, c, banks, ident, tag, evac_engine="scalar"):
        self.c = c
        self.banks = banks
        self.nb = len(banks)
        self.ident = ident
        self.tag = tag
        self.evac_engine = evac_engine
        self.t = 0
        self.tsem = "tr_" + tag
        self.esem = "tre_" + tag
        self.freed_at = []

    def emit(self, src_ap, dst_ap, extra_consumer=None):
        c = self.c
        bank = self.banks[self.t % self.nb]
        if self.t >= self.nb:
            c.wait("tensor", self.esem, self.freed_at[self.t - self.nb])
        c.op("tensor",
             lambda e, src_ap=src_ap, bank=bank: e.transpose(bank[:], src_ap, self.ident[:]),
             inc=(self.tsem, 1))
        c.wait(self.evac_engine, self.tsem, self.t + 1)
        if self.evac_engine == "scalar":
            c.op("scalar",
                 lambda e, dst_ap=dst_ap, bank=bank: e.copy(dst_ap, bank[:]),
                 inc=(self.esem, 1))
        else:
            c.op(self.evac_engine,
                 lambda e, dst_ap=dst_ap, bank=bank: e.tensor_copy(dst_ap, bank[:]),
                 inc=(self.esem, 1))
        if extra_consumer is not None:
            eng2, fn2 = extra_consumer
            c.wait(eng2, self.esem, c.val[self.esem])
            c.op(eng2, lambda e, bank=bank, fn2=fn2: fn2(e, bank), inc=(self.esem, 1))
        self.freed_at.append(c.val[self.esem])
        self.t += 1


def build_program(debug_outputs=(), phases=("a1", "a2", "a3", "b", "c")):
    nc = bass.Bass("TRN2", target_bir_lowering=False, detect_race_conditions=False)

    def scratch_kind(name):
        return "ExternalOutput" if name in debug_outputs else "Internal"

    x_d = nc.dram_tensor("x", [BB, S, D], F32, kind="ExternalInput").ap()
    wqkv_d = nc.dram_tensor("wqkv", [D, FQKV], F32, kind="ExternalInput").ap()
    bqkv_d = nc.dram_tensor("bqkv", [FQKV], F32, kind="ExternalInput").ap()
    wo_d = nc.dram_tensor("wo", [D, D], F32, kind="ExternalInput").ap()
    bo_d = nc.dram_tensor("bo", [D], F32, kind="ExternalInput").ap()
    rope_d = nc.dram_tensor("rope", [256, 32, 2], F32, kind="ExternalInput").ap()
    wg_d = nc.dram_tensor("wg", [I_FFN, D], F32, kind="ExternalInput").ap()
    wu_d = nc.dram_tensor("wu", [I_FFN, D], F32, kind="ExternalInput").ap()
    wd_d = nc.dram_tensor("wd", [D, I_FFN], F32, kind="ExternalInput").ap()
    out_d = nc.dram_tensor("out", [BB, S, D], F32, kind="ExternalOutput").ap()

    qt_s = nc.dram_tensor("qt_s", [BB, H, HD, S], F32, kind=scratch_kind("qt_s")).ap()
    kt_s = nc.dram_tensor("kt_s", [BB, H, HD, S], F32, kind=scratch_kind("kt_s")).ap()
    v_s = nc.dram_tensor("v_s", [BB, H, S, HD], F32, kind=scratch_kind("v_s")).ap()
    ao_s = nc.dram_tensor("ao_s", [BB, D, S], F32, kind=scratch_kind("ao_s")).ap()
    o1_s = nc.dram_tensor("o1_s", [BB, S, D], F32, kind=scratch_kind("o1_s")).ap()
    o2_s = nc.dram_tensor("o2_s", [BB, S, D], F32, kind=scratch_kind("o2_s")).ap()
    wq_dbg = nc.dram_tensor("wq_dbg", [6, 128, 11264], BF16, kind=scratch_kind("wq_dbg")).ap() \
        if "wq_dbg" in debug_outputs else None
    q_dbg = nc.dram_tensor("q_dbg", [2, 128, 1408], BF16, kind=scratch_kind("q_dbg")).ap() \
        if "q_dbg" in debug_outputs else None
    q_dbg2 = nc.dram_tensor("q_dbg2", [4, 128, 1408], F32, kind=scratch_kind("q_dbg2")).ap() \
        if "q_dbg2" in debug_outputs else None
    sw_dbg = nc.dram_tensor("sw_dbg", [4, 128, 1024], F32, kind=scratch_kind("sw_dbg")).ap() \
        if "sw_dbg" in debug_outputs else None
    ln_dbg = nc.dram_tensor("ln_dbg", [4, 128, NTILES], F32, kind=scratch_kind("ln_dbg")).ap() \
        if "ln_dbg" in debug_outputs else None
    hq_dbg = nc.dram_tensor("hq_dbg", [4, 128, 512], BF16, kind=scratch_kind("hq_dbg")).ap() \
        if "hq_dbg" in debug_outputs else None
    dp_s = nc.dram_tensor("dp_s", [BB, S, D], F32, kind=scratch_kind("dp_s")).ap()

    est = ExitStack()
    c = Ctx(nc, est)
    c.dummy = est.enter_context(nc.sbuf_tensor("barrier_dummy", [1, 12], F32))

    ident_f = est.enter_context(nc.sbuf_tensor("ident_f", [128, 128], F32))
    ident_b = est.enter_context(nc.sbuf_tensor("ident_b", [128, 128], BF16))
    mask_b = est.enter_context(nc.sbuf_tensor("mask_b", [128, 384], BF16))
    stats_m1 = est.enter_context(nc.sbuf_tensor("stats_m1", [128, NTILES], F32))
    stats_v1 = est.enter_context(nc.sbuf_tensor("stats_v1", [128, NTILES], F32))
    stats_m2 = est.enter_context(nc.sbuf_tensor("stats_m2", [128, NTILES], F32))
    stats_v2 = est.enter_context(nc.sbuf_tensor("stats_v2", [128, NTILES], F32))
    rstd1 = est.enter_context(nc.sbuf_tensor("rstd1", [128, NTILES], F32))
    rstd2 = est.enter_context(nc.sbuf_tensor("rstd2", [128, NTILES], F32))
    epsc = est.enter_context(nc.sbuf_tensor("epsc", [128, 1], F32))
    ubuf = est.enter_context(nc.sbuf_tensor("ubuf", [128, NTILES], F32))
    wbuf = est.enter_context(nc.sbuf_tensor("wbuf", [128, NTILES], F32))

    def emit_rstd(c, stats_m, stats_v, rstd_buf, fence_sem, fence_val):
        def vop(fn):
            c.op("vector", fn, inc=("v_rst", 1))
            c.wait("vector", "v_rst")
        c.wait("vector", fence_sem, fence_val)
        vop(lambda e: e.tensor_scalar(
            stats_m[:], stats_m[:], 2.0 ** -10, None, ALU.mult))
        vop(lambda e: e.tensor_tensor(wbuf[:], stats_m[:], stats_m[:], ALU.mult))
        vop(lambda e: e.tensor_scalar(
            ubuf[:], stats_v[:], 2.0 ** -10, EPS, ALU.mult, ALU.add))
        vop(lambda e: e.tensor_tensor(ubuf[:], ubuf[:], wbuf[:], ALU.subtract))
        c.op("vector", _nop, inc=("v_rst2", 1))
        c.wait("scalar", "v_rst2", c.val["v_rst2"])
        c.op("scalar", lambda e: e.activation(
            rstd_buf[:], ubuf[:], AF.Ln), inc=("s_rst", 1))
        c.wait("scalar", "s_rst")
        c.op("scalar", lambda e: e.activation(
            rstd_buf[:], rstd_buf[:], AF.Exp, scale=-0.5), inc=("s_rst", 1))
        c.wait("vector", "s_rst", c.val["s_rst"])
        vop(lambda e: e.tensor_tensor(wbuf[:], rstd_buf[:], rstd_buf[:], ALU.mult))
        vop(lambda e: e.tensor_tensor(wbuf[:], wbuf[:], ubuf[:], ALU.mult))
        vop(lambda e: e.tensor_scalar(
            wbuf[:], wbuf[:], -0.5, 1.5, ALU.mult, ALU.add))
        vop(lambda e: e.tensor_tensor(rstd_buf[:], rstd_buf[:], wbuf[:], ALU.mult))
        c.op("vector", _nop, inc=("v_rstd_done", 1))

    def phase_p0():
        c.op("gpsimd", lambda e: e.memset(ident_f[:], 1.0))
        c.op("gpsimd", lambda e: e.affine_select(
            ident_f[:], ident_f[:], [[-1, 128]], ALU.is_equal, 0.0,
            base=0, channel_multiplier=1))
        c.op("gpsimd", lambda e: e.memset(ident_b[:], 1.0))
        c.op("gpsimd", lambda e: e.affine_select(
            ident_b[:], ident_b[:], [[-1, 128]], ALU.is_equal, 0.0,
            base=0, channel_multiplier=1))
        c.op("gpsimd", lambda e: e.memset(epsc[:], EPS))
        c.op("gpsimd", lambda e: e.memset(mask_b[:], 0.0))
        c.op("gpsimd", lambda e: e.affine_select(
            mask_b[:, 0:128], mask_b[:, 0:128], [[1, 128]], ALU.is_ge, -1e30,
            base=0, channel_multiplier=-1))
        c.op("gpsimd", lambda e: e.affine_select(
            mask_b[:, 256:384], mask_b[:, 256:384], [[-1, 128]], ALU.is_ge, -1e30,
            base=0, channel_multiplier=1), inc=("g_const", 1))
        c.wait("tensor", "g_const", 1)
        c.wait("scalar", "g_const", 1)
        c.wait("vector", "g_const", 1)

    def phase_a1():
        with ExitStack() as ph:
            sb = lambda name, shape, dt=F32: ph.enter_context(nc.sbuf_tensor(name, shape, dt))
            wq_hi = sb("wq_hi", [128, 8, FQKV], BF16)
            wq_lo = sb("wq_lo", [128, 8, FQKV], BF16)
            wstage = [sb("wstage0", [128, FQKV])]
            brep = sb("brep", [128, FQKV])
            rop = sb("rop", [128, 128])
            tabs = {}
            for par in (0, 1):
                for cs in (0, 1):
                    for qk in (0, 1):
                        tabs[(par, cs, qk)] = sb(f"tab{par}{cs}{qk}", [128, 512])
            xt = [sb(f"xt{i}", [128, D]) for i in range(2)]
            xT_hi = sb("xT_hi", [128, D], BF16)
            xT_lo = sb("xT_lo", [128, D], BF16)
            xTf = sb("xTf", [128, D])
            qkv_sb = sb("qkv_sb", [128, FQKV])
            qr = sb("qr", [128, D])
            kr = sb("kr", [128, D])
            rtmp = sb("rtmp", [128, 512])
            qts = [sb(f"qts{i}", [128, D]) for i in range(2)]
            kts = [sb(f"kts{i}", [128, D]) for i in range(2)]
            vs_ = [sb(f"vs{i}", [128, D]) for i in range(2)]
            with ExitStack() as ps:
                pbanks = [ps.enter_context(nc.psum_tensor(f"ptrA1_{i}", [128, 128], F32))
                          for i in range(4)]
                pqkv = [ps.enter_context(nc.psum_tensor(f"pqkv{i}", [128, 512], F32))
                        for i in range(2)]
                pipe = TransposePipe(c, pbanks, ident_f, "a1")

                for fc in range(8):
                    buf = wstage[0]
                    if fc >= 1:
                        c.wait("sync", "v_wsplit", 2 * fc)
                    c.dma("sync", out=buf[:], in_=wqkv_d[fc * 128:(fc + 1) * 128, :], sem="ld_w")
                    c.wait("vector", "ld_w", (fc + 1) * 16)
                    c.op("vector", lambda e, fc=fc, buf=buf: e.tensor_copy(wq_hi[:, fc, :], buf[:]),
                         inc=("v_wsplit", 1))
                    c.op("vector", lambda e, fc=fc, buf=buf: e.tensor_tensor(
                        wq_lo[:, fc, :], buf[:], wq_hi[:, fc, :], ALU.subtract),
                        inc=("v_wsplit", 1))
                c.dma("sync", out=brep[:], in_=bqkv_d.unsqueeze(0).broadcast_to([128, FQKV]),
                      sem="ld_w")
                rflat = rope_d.rearrange("p j c -> p (j c)")
                c.dma("sync", out=rop[:, 0:64], in_=rflat[0:128, :], sem="ld_w")
                c.dma("sync", out=rop[:, 64:128], in_=rflat[128:256, :], sem="ld_w")
                c.wait("vector", "ld_w")
                for par in (0, 1):
                    for cs in (0, 1):
                        src = rop[:, 64 * par + cs: 64 * par + 64: 2]
                        src_b = src.unsqueeze(1).broadcast_to([128, H, 32])
                        for qk in (0, 1):
                            dst = tabs[(par, cs, qk)][:].rearrange("p (h j) -> p h j", j=32)
                            scale = 0.125 if qk == 0 else 1.0
                            c.op("vector", lambda e, dst=dst, src_b=src_b, scale=scale:
                                 e.tensor_scalar(dst, src_b, scale, None, ALU.mult),
                                 inc=("v_tab", 1))
                c.wait("tensor", "v_wsplit", 16)
                c.wait("tensor", "v_tab", 8)

                def load_x(j):
                    bb2, tj = divmod(j, NT)
                    if j >= 2:
                        c.wait("sync", "t_xdone", j - 1)
                    c.dma("sync", out=xt[j % 2][:],
                          in_=x_d[bb2, tj * 128:(tj + 1) * 128, :], sem=f"ld_x{j % 2}")

                load_x(0)
                for it in range(NTILES):
                    b, ti = divmod(it, NT)
                    par = ti % 2
                    xb = xt[it % 2]
                    if it + 1 < NTILES:
                        load_x(it + 1)
                    c.wait("tensor", f"ld_x{it % 2}", (it // 2 + 1) * 16)
                    for ch in range(8):
                        def lo_resid(e, bank, ch=ch):
                            return e.tensor_tensor(
                                xTf[:, ch * 128:(ch + 1) * 128], bank[:],
                                xT_hi[:, ch * 128:(ch + 1) * 128], ALU.subtract)
                        pipe.emit(xb[:, ch * 128:(ch + 1) * 128],
                                  xT_hi[:, ch * 128:(ch + 1) * 128],
                                  extra_consumer=("vector", lo_resid))
                        if ch == 7:
                            c.op("tensor", _nop, inc=("t_xdone", 1))
                    c.op("vector", lambda e: e.tensor_copy(xT_lo[:], xTf[:]), inc=("v_xlo", 1))

                    c.wait("tensor", "v_xlo", it + 1)
                    for fc in range(6):
                        pq = pqkv[fc % 2]
                        if it * 6 + fc >= 2:
                            c.wait("tensor", "v_qkvevac", it * 6 + fc - 1)
                        for ch in range(8):
                            c.op("tensor", lambda e, pq=pq, ch=ch, fc=fc: e.matmul(
                                pq[:], xT_hi[:, ch * 128:(ch + 1) * 128],
                                wq_hi[:, ch, fc * 512:(fc + 1) * 512],
                                start=(ch == 0), stop=False))
                            c.op("tensor", lambda e, pq=pq, ch=ch, fc=fc: e.matmul(
                                pq[:], xT_hi[:, ch * 128:(ch + 1) * 128],
                                wq_lo[:, ch, fc * 512:(fc + 1) * 512],
                                start=False, stop=False))
                            c.op("tensor", lambda e, pq=pq, ch=ch, fc=fc: e.matmul(
                                pq[:], xT_lo[:, ch * 128:(ch + 1) * 128],
                                wq_hi[:, ch, fc * 512:(fc + 1) * 512],
                                start=False, stop=(ch == 7)),
                                inc=(("t_qkv", 1) if ch == 7 else None))
                        c.wait("vector", "t_qkv", it * 6 + fc + 1)
                        if it >= 1 and fc == 0:
                            c.wait("vector", "s_vcopy", it)
                        c.op("vector", lambda e, pq=pq, fc=fc: e.tensor_tensor(
                            qkv_sb[:, fc * 512:(fc + 1) * 512], pq[:],
                            brep[:, fc * 512:(fc + 1) * 512], ALU.add),
                            inc=("v_qkvevac", 1))

                    if it >= 1:
                        c.wait("vector", "tr_a1", it * 24)
                    for qk, (dst, base) in enumerate(((qr, 0), (kr, D))):
                        x0 = qkv_sb[:, base + 0:base + D:2]
                        x1 = qkv_sb[:, base + 1:base + D:2]
                        ct = tabs[(par, 0, qk)][:]
                        st_ = tabs[(par, 1, qk)][:]
                        d0 = dst[:, 0:D:2]
                        d1 = dst[:, 1:D:2]
                        c.op("vector", lambda e, d0=d0, x0=x0, ct=ct: e.tensor_tensor(d0, x0, ct, ALU.mult))
                        c.op("vector", lambda e, x1=x1, st_=st_: e.tensor_tensor(rtmp[:], x1, st_, ALU.mult))
                        c.op("vector", lambda e, d0=d0: e.tensor_tensor(d0, d0, rtmp[:], ALU.subtract),
                             inc=("v_rope", 1))
                        c.op("vector", lambda e, d1=d1, x1=x1, ct=ct: e.tensor_tensor(d1, x1, ct, ALU.mult))
                        c.op("vector", lambda e, x0=x0, st_=st_: e.tensor_tensor(rtmp[:], x0, st_, ALU.mult))
                        c.op("vector", lambda e, d1=d1: e.tensor_tensor(d1, d1, rtmp[:], ALU.add),
                             inc=("v_rope", 1))

                    qb, kb, vb = qts[it % 2], kts[it % 2], vs_[it % 2]
                    if it >= 2:
                        c.wait("scalar", "st_qkt", (it - 1) * 48)
                    c.wait("tensor", "v_rope", (it + 1) * 4)
                    for src, dst in ((qr, qb), (kr, kb)):
                        for ch in range(8):
                            pipe.emit(src[:, ch * 128:(ch + 1) * 128],
                                      dst[:, ch * 128:(ch + 1) * 128])
                    c.wait("scalar", "v_qkvevac", (it + 1) * 6)
                    c.op("scalar", lambda e, vb=vb: e.copy(vb[:], qkv_sb[:, 2 * D:3 * D]),
                         inc=("s_vcopy", 1))
                    c.wait("sync", "tre_a1")
                    c.wait("sync", "s_vcopy", it + 1)
                    qdst = qt_s.rearrange("b h d s -> b (h d) s")[b, :, ti * 128:(ti + 1) * 128] \
                        .rearrange("(ch p) t -> p ch t", p=128)
                    kdst = kt_s.rearrange("b h d s -> b (h d) s")[b, :, ti * 128:(ti + 1) * 128] \
                        .rearrange("(ch p) t -> p ch t", p=128)
                    vdst = v_s[b, :, ti * 128:(ti + 1) * 128, :].transpose([1, 0, 2])
                    c.dma("sync", out=qdst, in_=qb[:].rearrange("p (ch t) -> p ch t", t=128), sem="st_qkt")
                    c.dma("sync", out=kdst, in_=kb[:].rearrange("p (ch t) -> p ch t", t=128), sem="st_qkt")
                    c.dma("sync", out=vdst, in_=vb[:].rearrange("p (h d) -> p h d", d=HD), sem="st_qkt")
                c.wait("sync", "st_qkt", c.val["st_qkt"])
        c.barrier("a1")

    def phase_a2():
        with ExitStack() as ph:
            sb = lambda name, shape, dt=F32: ph.enter_context(nc.sbuf_tensor(name, shape, dt))
            kt_sb = [sb(f"kt_sb{i}", [64, S]) for i in range(2)]
            qt_sb = [sb(f"qt_sb{i}", [64, S]) for i in range(2)]
            v_sb = [sb(f"v_sb{i}", [128, NT * HD]) for i in range(2)]
            junk = sb("junk", [128, 384])
            p_sb = sb("p_sb", [128, 384])
            pt_sb = sb("pt_sb", [128, 384])
            negm = [sb(f"negm{i}", [128, 1]) for i in range(2)]
            zrow = [sb(f"zrow{i}", [128, 1]) for i in range(2)]
            lnz = [sb(f"lnz{i}", [128, 1]) for i in range(2)]
            bias2 = [sb(f"bias2{i}", [128, 1]) for i in range(2)]
            aoh = [sb(f"aoh{i}", [64, S]) for i in range(2)]
            with ExitStack() as ps:
                pscore = [ps.enter_context(nc.psum_tensor(f"pscore{i}", [128, 384], F32))
                          for i in range(2)]
                ptr2 = [ps.enter_context(nc.psum_tensor(f"ptrA2_{i}", [128, 128], F32))
                        for i in range(3)]
                pao = [ps.enter_context(nc.psum_tensor(f"paoA2_{i}", [64, 128], F32))
                       for i in range(2)]
                pipe = TransposePipe(c, ptr2, ident_f, "a2", evac_engine="vector")

                def load_head(j):
                    b, h = divmod(j, H)
                    sem = f"ld_a2_{j % 2}"
                    if j >= 2:
                        c.wait("sync", "t_hd", j - 1)
                    c.dma("sync", out=kt_sb[j % 2][:], in_=kt_s[b, h], sem=sem)
                    c.dma("sync", out=qt_sb[j % 2][:], in_=qt_s[b, h], sem=sem)
                    c.dma("sync", out=v_sb[j % 2][:].rearrange("p (kt d) -> p kt d", d=HD),
                          in_=v_s[b, h].rearrange("(kt p) d -> p kt d", p=128), sem=sem)

                load_head(0)
                for j in range(BB * H):
                    b, h = divmod(j, H)
                    kt, qt, vt, ao = kt_sb[j % 2], qt_sb[j % 2], v_sb[j % 2], aoh[j % 2]
                    if j + 1 < BB * H:
                        load_head(j + 1)
                    c.wait("tensor", f"ld_a2_{j % 2}", (j // 2 + 1) * 48)
                    if j >= 2:
                        c.wait("scalar", "st_ao", (j - 1) * 16)
                    for i in range(NT):
                        g = j * NT + i
                        kt0 = max(i - 1, 0)
                        kt1 = min(i + 1, NT - 1)
                        nch = kt1 - kt0 + 1
                        kcols = nch * 128
                        psc = pscore[i % 2]
                        moff = 128 if i == 0 else 0
                        if g >= 2:
                            c.wait("tensor", "s_exp2", g - 1)
                        c.op("tensor", lambda e, psc=psc, qt=qt, i=i, kt=kt, kt0=kt0, kcols=kcols:
                             e.matmul(psc[:, 0:kcols], qt[:, i * 128:(i + 1) * 128],
                                      kt[:, kt0 * 128:kt0 * 128 + kcols],
                                      start=True, stop=False))
                        c.op("tensor", lambda e, psc=psc, kcols=kcols, moff=moff:
                             e.matmul(psc[:, 0:kcols], ident_b[:],
                                      mask_b[:, moff:moff + kcols],
                                      start=False, stop=True),
                             inc=("t_qk", 1))
                        nm, zr, lz, b2 = negm[g % 2], zrow[g % 2], lnz[g % 2], bias2[g % 2]
                        c.wait("vector", "t_qk", g + 1)
                        if g >= 1:
                            c.wait("vector", "s_exp1", g - 1)
                        c.op("vector", lambda e, psc=psc, kcols=kcols, nm=nm:
                             e.reduce_max(nm[:], psc[:, 0:kcols], AX.X, negate=True),
                             inc=("v_max", 1))
                        c.wait("scalar", "v_max", g + 1)
                        c.op("scalar", lambda e, psc=psc, kcols=kcols, nm=nm, zr=zr:
                             e.activation(junk[:, 0:kcols], psc[:, 0:kcols], AF.Exp,
                                          bias=nm[:, 0:1], accum_out=zr[:]), inc=("s_e1f", 1))
                        c.wait("scalar", "s_e1f")
                        c.op("scalar", lambda e, lz=lz, zr=zr: e.activation(lz[:], zr[:], AF.Ln),
                             inc=("s_exp1", 1))
                        c.wait("vector", "s_exp1", g + 1)
                        if g >= 1:
                            c.wait("vector", "s_exp2", g - 1)
                        c.op("vector", lambda e, b2=b2, nm=nm, lz=lz:
                             e.tensor_tensor(b2[:], nm[:], lz[:], ALU.subtract),
                             inc=("v_bias2", 1))
                        c.wait("scalar", "v_bias2", g + 1)
                        if g >= 1:
                            c.wait("scalar", "tr_a2", pipe.t)
                        c.op("scalar", lambda e, psc=psc, kcols=kcols, b2=b2:
                             e.activation(p_sb[:, 0:kcols], psc[:, 0:kcols], AF.Exp,
                                          bias=b2[:, 0:1]),
                             inc=("s_exp2", 1))
                        c.wait("tensor", "s_exp2", g + 1)
                        if g >= 1:
                            c.wait("tensor", "t_pv", g)
                        for ch in range(nch):
                            pipe.emit(p_sb[:, ch * 128:(ch + 1) * 128],
                                      pt_sb[:, ch * 128:(ch + 1) * 128])
                        pv = pao[i % 2]
                        if g >= 2:
                            c.wait("tensor", "s_ao", g - 1)
                        c.wait("tensor", "tre_a2", pipe.t)
                        for ch in range(nch):
                            ktile = kt0 + ch
                            c.op("tensor", lambda e, pv=pv, vt=vt, ktile=ktile, ch=ch, nch=nch:
                                 e.matmul(pv[:], vt[:, ktile * HD:(ktile + 1) * HD],
                                          pt_sb[:, ch * 128:(ch + 1) * 128],
                                          start=(ch == 0), stop=(ch == nch - 1)),
                                 inc=(("t_pv", 1) if ch == nch - 1 else None))
                        c.wait("scalar", "t_pv", g + 1)
                        c.op("scalar", lambda e, ao=ao, pv=pv, i=i:
                             e.copy(ao[:, i * 128:(i + 1) * 128], pv[:]),
                             inc=("s_ao", 1))
                    c.wait("sync", "s_ao", (j + 1) * NT)
                    c.dma("sync", out=ao_s[b, h * HD:(h + 1) * HD, :], in_=ao[:], sem="st_ao")
                    c.op("tensor", _nop, inc=("t_hd", 1))
                c.wait("sync", "st_ao", c.val["st_ao"])
        c.barrier("a2")

    def phase_a3():
        with ExitStack() as ph:
            sb = lambda name, shape, dt=F32: ph.enter_context(nc.sbuf_tensor(name, shape, dt))
            wo_hi = sb("wo_hi", [128, 8, D], BF16)
            wo_lo = sb("wo_lo", [128, 8, D], BF16)
            wstage3 = sb("wstage3", [128, D])
            borep = sb("borep", [128, D])
            aot = [sb(f"aot{i}", [128, D]) for i in range(2)]
            aot_hi = sb("aot_hi", [128, D], BF16)
            aot_lo = sb("aot_lo", [128, D], BF16)
            xt3 = [sb(f"xt3_{i}", [128, D]) for i in range(2)]
            o1t = [sb(f"o1t{i}", [128, D]) for i in range(2)]
            junk3 = sb("junk3", [128, D])
            junk3b = sb("junk3b", [128, D])
            acc3 = sb("acc3", [128, 2])
            nm3 = [sb(f"nm3_{i}", [128, 1]) for i in range(2)]
            with ExitStack() as ps:
                pwo = [ps.enter_context(nc.psum_tensor(f"pwo{i}", [128, 512], F32))
                       for i in range(2)]
                for fc in range(8):
                    if fc >= 1:
                        c.wait("sync", "v_wsplit3", 2 * fc)
                    c.dma("sync", out=wstage3[:], in_=wo_d[fc * 128:(fc + 1) * 128, :], sem="ld_w3")
                    c.wait("vector", "ld_w3", (fc + 1) * 16)
                    c.op("vector", lambda e, fc=fc: e.tensor_copy(wo_hi[:, fc, :], wstage3[:]),
                         inc=("v_wsplit3", 1))
                    c.op("vector", lambda e, fc=fc: e.tensor_tensor(
                        wo_lo[:, fc, :], wstage3[:], wo_hi[:, fc, :], ALU.subtract),
                        inc=("v_wsplit3", 1))
                c.dma("sync", out=borep[:], in_=bo_d.unsqueeze(0).broadcast_to([128, D]),
                      sem="ld_w3")
                c.wait("tensor", "v_wsplit3", 16)
                c.wait("vector", "ld_w3")

                for it in range(NTILES):
                    b, ti = divmod(it, NT)
                    ab, xb, ob = aot[it % 2], xt3[it % 2], o1t[it % 2]
                    if it >= 2:
                        c.wait("sync", "t_wo", (it - 1) * 2)
                    asrc = ao_s[b, :, ti * 128:(ti + 1) * 128].rearrange(
                        "(ch p) t -> p ch t", p=128)
                    c.dma("sync", out=ab[:].rearrange("p (ch t) -> p ch t", t=128),
                          in_=asrc, sem=f"ld_ao{it % 2}")
                    c.dma("sync", out=xb[:], in_=x_d[b, ti * 128:(ti + 1) * 128, :],
                          sem=f"ld_ao{it % 2}")
                    c.wait("scalar", f"ld_ao{it % 2}", (it // 2 + 1) * 32)
                    if it >= 1:
                        c.wait("scalar", "t_wo", it * 2)
                    c.op("scalar", lambda e, ab=ab: e.copy(aot_hi[:], ab[:]), inc=("s_ahi", 1))
                    c.wait("vector", "s_ahi", it + 1)
                    c.op("vector", lambda e, ab=ab: e.tensor_tensor(
                        aot_lo[:], ab[:], aot_hi[:], ALU.subtract), inc=("v_alo", 1))
                    c.wait("tensor", "v_alo", it + 1)
                    for fc in range(2):
                        pw = pwo[fc]
                        if it >= 1:
                            c.wait("tensor", "v_oevac", it * 2 + fc - 1)
                        for ch in range(8):
                            c.op("tensor", lambda e, pw=pw, ch=ch, fc=fc: e.matmul(
                                pw[:], aot_hi[:, ch * 128:(ch + 1) * 128],
                                wo_hi[:, ch, fc * 512:(fc + 1) * 512],
                                start=(ch == 0), stop=False))
                            c.op("tensor", lambda e, pw=pw, ch=ch, fc=fc: e.matmul(
                                pw[:], aot_hi[:, ch * 128:(ch + 1) * 128],
                                wo_lo[:, ch, fc * 512:(fc + 1) * 512],
                                start=False, stop=False))
                            c.op("tensor", lambda e, pw=pw, ch=ch, fc=fc: e.matmul(
                                pw[:], aot_lo[:, ch * 128:(ch + 1) * 128],
                                wo_hi[:, ch, fc * 512:(fc + 1) * 512],
                                start=False, stop=(ch == 7)),
                                inc=(("t_wo", 1) if ch == 7 else None))
                        c.wait("vector", "t_wo", it * 2 + fc + 1)
                        if it >= 2 and fc == 0:
                            c.wait("vector", "st_o1", (it - 1) * 16)
                        c.op("vector", lambda e, pw=pw, fc=fc: e.tensor_tensor(
                            junk3[:, fc * 512:(fc + 1) * 512], pw[:],
                            borep[:, fc * 512:(fc + 1) * 512], ALU.add))
                        c.op("vector", lambda e, ob=ob, xb=xb, fc=fc: e.tensor_tensor(
                            ob[:, fc * 512:(fc + 1) * 512],
                            junk3[:, fc * 512:(fc + 1) * 512],
                            xb[:, fc * 512:(fc + 1) * 512], ALU.add),
                            inc=("v_oevac", 1))
                    c.op("vector", lambda e, ob=ob, it=it: e.tensor_reduce(
                        stats_m1[:, it:it + 1].unsqueeze(2),
                        ob[:].unsqueeze(1), AX.X, ALU.add), inc=("v_red", 1))
                    c.op("vector", lambda e, ob=ob: e.tensor_tensor(
                        junk3b[:], ob[:], ob[:], ALU.mult))
                    c.op("vector", lambda e, it=it: e.tensor_reduce(
                        stats_v1[:, it:it + 1].unsqueeze(2),
                        junk3b[:].unsqueeze(1), AX.X, ALU.add), inc=("v_red", 1))
                    c.wait("sync", "v_oevac", (it + 1) * 2)
                    c.dma("sync", out=o1_s[b, ti * 128:(ti + 1) * 128, :], in_=ob[:],
                          sem="st_o1")
                emit_rstd(c, stats_m1, stats_v1, rstd1, "v_red", None)
                c.op("vector", _nop, inc=("v_m1", 1))
                if "ln_dbg" in debug_outputs:
                    c.wait("sync", "v_rstd_done", 1)
                    c.wait("sync", "v_m1", 1)
                    c.dma("sync", out=ln_dbg[0, :, :], in_=stats_m1[:], sem="st_dbg")
                    c.dma("sync", out=ln_dbg[1, :, :], in_=stats_v1[:], sem="st_dbg")
                    c.dma("sync", out=ln_dbg[2, :, :], in_=ubuf[:], sem="st_dbg")
                    c.dma("sync", out=ln_dbg[3, :, :], in_=rstd1[:], sem="st_dbg")
                    c.wait("sync", "st_dbg", c.val["st_dbg"])
                c.wait("sync", "st_o1", c.val["st_o1"])
        c.barrier("a3")

    def phase_b():
        QN = 1408
        with ExitStack() as ph:
            sb = lambda name, shape, dt=F32: ph.enter_context(nc.sbuf_tensor(name, shape, dt))
            wgT_hi = sb("wgT_hi", [128, 8 * I_HALF], BF16)
            wgT_lo = sb("wgT_lo", [128, 8 * I_HALF], BF16)
            wuT_hi = sb("wuT_hi", [128, 8 * I_HALF], BF16)
            wuT_lo = sb("wuT_lo", [128, 8 * I_HALF], BF16)
            wdT_hi = sb("wdT_hi", [128, 11 * D], BF16)
            wdT_lo = sb("wdT_lo", [128, 11 * D], BF16)
            qa = sb("qa", [128, QN])
            qsgn = sb("qsgn", [128, QN], BF16)
            qssg = sb("qssg", [128, QN])
            qcA = sb("qcA", [128, QN], BF16)
            qcB = sb("qcB", [128, QN], BF16)
            qmr = sb("qmr", [128, 128])
            qs_ = sb("qs_", [128, 128])
            qsx = sb("qsx", [128, QN])
            qh_ = sb("qh_", [128, QN], BF16)
            ql_ = sb("ql_", [128, QN], BF16)
            wstg = [sb("wstgB0", [128, QN])]
            o1b = [sb(f"o1b{i}", [128, D]) for i in range(2)]
            xqT_hi = sb("xqT_hi", [128, D], BF16)
            xqT_lo = sb("xqT_lo", [128, D], BF16)
            hbuf = sb("hbuf", [128, 512])
            gact = sb("gact", [128, 512])
            hqT_hi = sb("hqT_hi", [128, 512], BF16)
            hqT_lo = sb("hqT_lo", [128, 512], BF16)
            dpt = [sb(f"dpt{i}", [128, D]) for i in range(2)]
            o2t = [sb(f"o2t{i}", [128, D]) for i in range(2)]
            accb = sb("accb", [128, 2])
            nm2 = [sb(f"nm2_{i}", [128, 1]) for i in range(2)]

            def emit_quant(src_ap, n):
                nb = n // 16
                a3d = qa[:, 0:n].rearrange("p (nb w) -> p nb w", w=16)
                sb3 = qs_[:, 0:nb].unsqueeze(2).broadcast_to([128, nb, 16])
                sx3 = qsx[:, 0:n].rearrange("p (nb w) -> p nb w", w=16)
                c.wait("scalar", "q_v")
                c.op("scalar", lambda e, src_ap=src_ap, n=n: e.activation(
                    qa[:, 0:n], src_ap, AF.Abs), inc=("q_s", 1))
                c.op("scalar", lambda e, src_ap=src_ap, n=n: e.activation(
                    qsgn[:, 0:n], src_ap, AF.Sign), inc=("q_s", 1))
                c.wait("vector", "q_s")
                c.op("vector", lambda e, a3d=a3d, nb=nb: e.tensor_reduce(
                    qmr[:, 0:nb].unsqueeze(2), a3d, AX.X, ALU.max), inc=("q_v", 1))
                c.wait("vector", "q_v")
                c.op("vector", lambda e, nb=nb: e.tensor_scalar(
                    qs_[:, 0:nb], qmr[:, 0:nb], 1.0 / 6.0, 1e-12, ALU.mult, ALU.max),
                    inc=("q_v", 1))
                c.wait("vector", "q_v")
                c.op("vector", lambda e, sx3=sx3, sb3=sb3: e.tensor_scalar(
                    sx3, sb3, 1.0, None, ALU.mult), inc=("q_v", 1))
                c.wait("vector", "q_v")
                c.op("vector", lambda e, n=n: e.scalar_tensor_tensor(
                    qcA[:, 0:n], qsx[:, 0:n], MIDS[0], qa[:, 0:n], ALU.mult, ALU.is_lt))
                for j in (1, 2, 3):
                    c.op("vector", lambda e, n=n, j=j: e.scalar_tensor_tensor(
                        qcB[:, 0:n], qsx[:, 0:n], MIDS[j], qa[:, 0:n], ALU.mult, ALU.is_lt))
                    c.op("vector", lambda e, n=n: e.tensor_tensor(
                        qcA[:, 0:n], qcA[:, 0:n], qcB[:, 0:n], ALU.add))
                c.op("vector", lambda e, n=n: e.scalar_tensor_tensor(
                    qssg[:, 0:n], qsx[:, 0:n], MIDS[4], qa[:, 0:n], ALU.mult, ALU.is_lt))
                c.op("vector", lambda e, n=n: e.scalar_tensor_tensor(
                    qcB[:, 0:n], qsx[:, 0:n], MIDS[5], qa[:, 0:n], ALU.mult, ALU.is_lt))
                c.op("vector", lambda e, n=n: e.tensor_tensor(
                    qssg[:, 0:n], qssg[:, 0:n], qcB[:, 0:n], ALU.add))
                c.op("vector", lambda e, n=n: e.scalar_tensor_tensor(
                    qcA[:, 0:n], qcA[:, 0:n], 0.5, qssg[:, 0:n], ALU.mult, ALU.add))
                c.op("vector", lambda e, n=n: e.scalar_tensor_tensor(
                    qcB[:, 0:n], qsx[:, 0:n], MIDS[6], qa[:, 0:n], ALU.mult, ALU.is_lt))
                c.op("vector", lambda e, n=n: e.scalar_tensor_tensor(
                    qcA[:, 0:n], qcB[:, 0:n], 2.0, qcA[:, 0:n], ALU.mult, ALU.add),
                    inc=("q_v", 1))
                c.wait("vector", "q_v")
                c.op("vector", lambda e, n=n: e.tensor_tensor(
                    qssg[:, 0:n], qsgn[:, 0:n], qsx[:, 0:n], ALU.mult))
                c.op("vector", lambda e, n=n: e.tensor_tensor(
                    qssg[:, 0:n], qcA[:, 0:n], qssg[:, 0:n], ALU.mult))
                c.op("vector", lambda e, n=n: e.tensor_copy(qh_[:, 0:n], qssg[:, 0:n]),
                     inc=("q_v", 1))
                c.op("vector", lambda e, n=n: e.tensor_tensor(
                    ql_[:, 0:n], qssg[:, 0:n], qh_[:, 0:n], ALU.subtract), inc=("q_v", 1))
                c.wait("vector", "q_v")

            with ExitStack() as ps:
                pd = [ps.enter_context(nc.psum_tensor(f"pdn{i}", [128, D], F32))
                      for i in range(2)]
                pg = ps.enter_context(nc.psum_tensor("pgate", [128, 512], F32))
                pu = ps.enter_context(nc.psum_tensor("pup", [128, 512], F32))
                ptrB = [ps.enter_context(nc.psum_tensor(f"ptrB{i}", [128, 128], BF16))
                        for i in range(2)]
                pipe = TransposePipe(c, ptrB, ident_b, "b")

                W_SUB = [512] if "onesub" in debug_outputs else [512, 512, 384]
                x1_qs = {}
                for half in range(2):
                    if half == 1:
                        c.barrier("b_half")
                    wt_jobs = []
                    for rt in range(11):
                        wt_jobs.append(("g", rt, wg_d[half * I_HALF + rt * 128:
                                                     half * I_HALF + (rt + 1) * 128, :], D))
                    for rt in range(11):
                        wt_jobs.append(("u", rt, wu_d[half * I_HALF + rt * 128:
                                                     half * I_HALF + (rt + 1) * 128, :], D))
                    for rt in range(8):
                        wt_jobs.append(("d", rt, wd_d[rt * 128:(rt + 1) * 128,
                                                      half * I_HALF:(half + 1) * I_HALF], I_HALF))
                    qs_after = []
                    for jn, (kind, rt, src, n) in enumerate(wt_jobs):
                        buf = wstg[0]
                        if jn >= 1:
                            c.wait("sync", "q_s", qs_after[jn - 1])
                        c.dma("sync", out=buf[:, 0:n], in_=src, sem="ld_wb")
                        c.wait("scalar", "ld_wb", c.val["ld_wb"])
                        emit_quant(buf[:, 0:n], n)
                        qs_after.append(c.val["q_s"])
                        c.wait("tensor", "q_v")
                        if q_dbg2 is not None and half == 0 and jn == 0:
                            c.wait("sync", "q_v")
                            c.dma("sync", out=q_dbg2[0, :, 0:128], in_=qmr[:], sem="st_dbg")
                            c.dma("sync", out=q_dbg2[1, :, 0:128], in_=qs_[:], sem="st_dbg")
                            c.dma("sync", out=q_dbg2[2, :, 0:128], in_=qr_[:], sem="st_dbg")
                            c.dma("sync", out=q_dbg2[3, :, 0:n], in_=qa[:, 0:n], sem="st_dbg")
                            c.wait("sync", "st_dbg", c.val["st_dbg"])
                            c.barrier("dbgb")
                        if q_dbg is not None and half == 0 and jn == 0:
                            c.wait("sync", "q_v")
                            c.dma("sync", out=q_dbg[0, :, 0:n], in_=qh_[:, 0:n], sem="st_dbg")
                            c.dma("sync", out=q_dbg[1, :, 0:n], in_=ql_[:, 0:n], sem="st_dbg")
                            c.wait("sync", "st_dbg", 32)
                        nch = n // 128
                        for ch in range(nch):
                            for hl, (qsrc, wdst) in enumerate((
                                    (qh_, {"g": wgT_hi, "u": wuT_hi, "d": wdT_hi}[kind]),
                                    (ql_, {"g": wgT_lo, "u": wuT_lo, "d": wdT_lo}[kind]))):
                                if kind == "d":
                                    dst = wdst[:, ch * D + rt * 128: ch * D + (rt + 1) * 128]
                                else:
                                    dst = wdst[:, ch * I_HALF + rt * 128: ch * I_HALF + (rt + 1) * 128]
                                pipe.emit(qsrc[:, ch * 128:(ch + 1) * 128], dst)
                        c.wait("vector", "tr_b", pipe.t)
                    c.wait("tensor", "tre_b")
                    if wq_dbg is not None and half == 0:
                        c.wait("sync", "tre_b")
                        for di, wt in enumerate((wgT_hi, wgT_lo, wuT_hi, wuT_lo,
                                                 wdT_hi, wdT_lo)):
                            c.dma("sync", out=wq_dbg[di], in_=wt[:], sem="st_dbg")
                        c.wait("sync", "st_dbg", 96)

                    for it in range(NTILES):
                        b, ti = divmod(it, NT)
                        ob = o1b[it % 2]
                        xb = ob
                        pdn = pd[it % 2]
                        gidx = half * NTILES + it
                        if gidx >= 2:
                            c.wait("sync", "q_s", x1_qs.get(gidx - 2, 0))
                            if half == 1 and it >= 2:
                                c.wait("sync", "v_o2", (it - 1) * 2)
                        c.dma("sync", out=ob[:], in_=o1_s[b, ti * 128:(ti + 1) * 128, :],
                              sem=f"ld_o1{it % 2}")
                        if half == 1:
                            db = dpt[it % 2]
                            if it >= 2:
                                c.wait("sync", "v_o2", (it - 1) * 2)
                            c.dma("sync", out=db[:], in_=dp_s[b, ti * 128:(ti + 1) * 128, :],
                                  sem=f"ld_dp{it % 2}")
                        if gidx == 0:
                            c.wait("vector", "v_rstd_done", 1)
                            c.wait("vector", "v_m1", 1)
                        c.wait("vector", f"ld_o1{it % 2}", ((half * NTILES + it) // 2 + 1) * 16)
                        c.op("vector", lambda e, ob=ob, it=it: e.tensor_scalar(
                            ob[:], ob[:], stats_m1[:, it:it + 1], rstd1[:, it:it + 1],
                            ALU.subtract, ALU.mult), inc=("v_x1", 1))
                        c.wait("scalar", "v_x1", gidx + 1)
                        emit_quant(xb[:], D)
                        x1_qs[gidx] = c.val["q_s"]
                        c.wait("tensor", "q_v")
                        for ch in range(8):
                            pipe.emit(qh_[:, ch * 128:(ch + 1) * 128],
                                      xqT_hi[:, ch * 128:(ch + 1) * 128])
                            pipe.emit(ql_[:, ch * 128:(ch + 1) * 128],
                                      xqT_lo[:, ch * 128:(ch + 1) * 128])
                        c.wait("vector", "tr_b", pipe.t)
                        if gidx >= 2:
                            if half == 0:
                                c.wait("tensor", "s_dpe", (gidx - 1) * 2)
                            elif it >= 2:
                                c.wait("tensor", "v_o2", (gidx - NTILES - 1) * 2)
                        icol = 0
                        for sub, wsub in enumerate(W_SUB):
                            c.wait("tensor", "tre_b")
                            nsub = len(W_SUB)
                            if gidx * nsub + sub >= 1:
                                c.wait("tensor", "s_silu", gidx * nsub + sub)
                                c.wait("tensor", "v_h", gidx * nsub + sub)
                            for ch in range(8):
                                co = ch * I_HALF + icol
                                c.op("tensor", lambda e, ch=ch, co=co, wsub=wsub: e.matmul(
                                    pg[:, 0:wsub], xqT_hi[:, ch * 128:(ch + 1) * 128],
                                    wgT_hi[:, co:co + wsub], start=(ch == 0), stop=False))
                                c.op("tensor", lambda e, ch=ch, co=co, wsub=wsub: e.matmul(
                                    pg[:, 0:wsub], xqT_hi[:, ch * 128:(ch + 1) * 128],
                                    wgT_lo[:, co:co + wsub], start=False, stop=False))
                                c.op("tensor", lambda e, ch=ch, co=co, wsub=wsub: e.matmul(
                                    pg[:, 0:wsub], xqT_lo[:, ch * 128:(ch + 1) * 128],
                                    wgT_hi[:, co:co + wsub], start=False, stop=(ch == 7)),
                                    inc=(("t_g", 1) if ch == 7 else None))
                            for ch in range(8):
                                co = ch * I_HALF + icol
                                c.op("tensor", lambda e, ch=ch, co=co, wsub=wsub: e.matmul(
                                    pu[:, 0:wsub], xqT_hi[:, ch * 128:(ch + 1) * 128],
                                    wuT_hi[:, co:co + wsub], start=(ch == 0), stop=False))
                                c.op("tensor", lambda e, ch=ch, co=co, wsub=wsub: e.matmul(
                                    pu[:, 0:wsub], xqT_hi[:, ch * 128:(ch + 1) * 128],
                                    wuT_lo[:, co:co + wsub], start=False, stop=False))
                                c.op("tensor", lambda e, ch=ch, co=co, wsub=wsub: e.matmul(
                                    pu[:, 0:wsub], xqT_lo[:, ch * 128:(ch + 1) * 128],
                                    wuT_hi[:, co:co + wsub], start=False, stop=(ch == 7)),
                                    inc=(("t_u", 1) if ch == 7 else None))
                            if sw_dbg is not None and gidx == DBG_GIDX and sub == 0:
                                c.wait("sync", "v_x1", 1)
                                c.dma("sync", out=sw_dbg[0], in_=xb[:], sem="st_dbg")
                                c.wait("vector", "t_g", 1)
                                c.op("vector", lambda e: e.tensor_copy(qssg[:, 0:512], pg[:, 0:512]),
                                     inc=("v_dbg", 1))
                                c.wait("sync", "v_dbg", 1)
                                c.dma("sync", out=sw_dbg[1, :, 0:512], in_=qssg[:, 0:512], sem="st_dbg")
                                c.wait("vector", "t_u", 1)
                                c.op("vector", lambda e: e.tensor_copy(qssg[:, 512:1024], pu[:, 0:512]),
                                     inc=("v_dbg", 1))
                                c.wait("sync", "v_dbg", 2)
                                c.dma("sync", out=sw_dbg[2, :, 0:512], in_=qssg[:, 512:1024], sem="st_dbg")
                                c.wait("sync", "st_dbg", c.val["st_dbg"])
                                c.barrier("dbg2")
                            c.wait("scalar", "t_g", gidx * nsub + sub + 1)
                            if sw_dbg is not None and gidx == DBG_GIDX and sub == 0:
                                dbg_h = True
                            else:
                                dbg_h = False
                            c.wait("scalar", "v_h")
                            c.op("scalar", lambda e, wsub=wsub: e.activation(
                                gact[:, 0:wsub], pg[:, 0:wsub], AF.Silu), inc=("s_silu", 1))
                            c.wait("vector", "t_u", gidx * nsub + sub + 1)
                            c.wait("vector", "s_silu", gidx * nsub + sub + 1)
                            c.wait("vector", "q_s")
                            c.op("vector", lambda e, wsub=wsub: e.tensor_tensor(
                                hbuf[:, 0:wsub], gact[:, 0:wsub], pu[:, 0:wsub], ALU.mult),
                                inc=("v_h", 1))
                            if dbg_h:
                                c.wait("sync", "v_h", 1)
                                c.dma("sync", out=sw_dbg[3, :, 0:wsub], in_=hbuf[:, 0:wsub],
                                      sem="st_dbg")
                                c.wait("sync", "st_dbg", c.val["st_dbg"])
                                c.barrier("dbg3")
                            c.wait("scalar", "v_h")
                            emit_quant(hbuf[:, 0:wsub], wsub)
                            if dbg_h and q_dbg2 is not None:
                                c.wait("sync", "q_v")
                                c.dma("sync", out=q_dbg2[0, :, 0:128], in_=qmr[:], sem="st_dbg")
                                c.dma("sync", out=q_dbg2[1, :, 0:128], in_=qs_[:], sem="st_dbg")
                                c.dma("sync", out=q_dbg2[2, :, 0:128], in_=qr_[:], sem="st_dbg")
                                c.dma("sync", out=q_dbg2[3, :, 0:wsub], in_=qa[:, 0:wsub], sem="st_dbg")
                                c.wait("sync", "st_dbg", c.val["st_dbg"])
                                c.barrier("dbg5")
                            nci = wsub // 128
                            c.wait("tensor", "q_v")
                            for ci in range(nci):
                                pipe.emit(qh_[:, ci * 128:(ci + 1) * 128],
                                          hqT_hi[:, ci * 128:(ci + 1) * 128])
                                pipe.emit(ql_[:, ci * 128:(ci + 1) * 128],
                                          hqT_lo[:, ci * 128:(ci + 1) * 128])
                            if hq_dbg is not None and gidx == DBG_GIDX and sub == 0:
                                c.wait("sync", "q_v")
                                c.dma("sync", out=hq_dbg[0, :, 0:wsub], in_=qh_[:, 0:wsub], sem="st_dbg")
                                c.dma("sync", out=hq_dbg[1, :, 0:wsub], in_=ql_[:, 0:wsub], sem="st_dbg")
                                c.wait("sync", "tre_b")
                                c.dma("sync", out=hq_dbg[2, :, 0:wsub], in_=hqT_hi[:, 0:wsub], sem="st_dbg")
                                c.dma("sync", out=hq_dbg[3, :, 0:wsub], in_=hqT_lo[:, 0:wsub], sem="st_dbg")
                                c.wait("sync", "st_dbg", c.val["st_dbg"])
                                c.barrier("dbg4")
                            c.wait("vector", "tr_b", pipe.t)
                            c.wait("tensor", "tre_b")
                            first = (sub == 0)
                            last = (sub == len(W_SUB) - 1)
                            for ci in range(nci):
                                cig = (icol // 128) + ci
                                for hc in range(2):
                                    for ps_, (lhs, rhs) in enumerate((
                                            (hqT_hi, wdT_hi), (hqT_hi, wdT_lo), (hqT_lo, wdT_hi))):
                                        st = first and ci == 0 and ps_ == 0
                                        sp = last and ci == nci - 1 and ps_ == 2
                                        c.op("tensor", lambda e, pdn=pdn, lhs=lhs, rhs=rhs,
                                             ci=ci, cig=cig, hc=hc, st=st, sp=sp: e.matmul(
                                                 pdn[:, hc * 512:(hc + 1) * 512],
                                                 lhs[:, ci * 128:(ci + 1) * 128],
                                                 rhs[:, cig * D + hc * 512:cig * D + (hc + 1) * 512],
                                                 start=st, stop=sp),
                                             inc=(("t_dn", 1) if (sp and hc == 1) else None))
                            icol += wsub
                        if half == 0:
                            dpo = dpt[it % 2]
                            if it >= 2:
                                c.wait("scalar", "st_dp", (it - 1) * 16)
                            c.wait("scalar", "t_dn", gidx + 1)
                            c.op("scalar", lambda e, dpo=dpo, pdn=pdn: e.copy(
                                dpo[:, 0:512], pdn[:, 0:512]), inc=("s_dpe", 1))
                            c.op("scalar", lambda e, dpo=dpo, pdn=pdn: e.copy(
                                dpo[:, 512:D], pdn[:, 512:D]), inc=("s_dpe", 1))
                            c.wait("sync", "s_dpe", (it + 1) * 2)
                            c.dma("sync", out=dp_s[b, ti * 128:(ti + 1) * 128, :],
                                  in_=dpo[:], sem="st_dp")
                        else:
                            o2b = o2t[it % 2]
                            nm = nm2[it % 2]
                            c.wait("vector", "t_dn", gidx + 1)
                            c.wait("vector", f"ld_dp{it % 2}", (it // 2 + 1) * 16)
                            if it >= 2:
                                c.wait("vector", "st_o2", (it - 1) * 16)
                                c.wait("vector", "s_sq2", it - 1)
                            dbl = dpt[it % 2]
                            for hc in range(2):
                                c.op("vector", lambda e, pdn=pdn, dbl=dbl, o2b=o2b, hc=hc:
                                     e.tensor_tensor(
                                         o2b[:, hc * 512:(hc + 1) * 512],
                                         pdn[:, hc * 512:(hc + 1) * 512],
                                         dbl[:, hc * 512:(hc + 1) * 512], ALU.add))
                                c.op("vector", lambda e, o2b=o2b, xb=xb, hc=hc: e.tensor_tensor(
                                    o2b[:, hc * 512:(hc + 1) * 512],
                                    o2b[:, hc * 512:(hc + 1) * 512],
                                    xb[:, hc * 512:(hc + 1) * 512], ALU.add),
                                    inc=("v_o2", 1))
                            c.op("vector", lambda e, o2b=o2b, it=it: e.tensor_reduce(
                                stats_m2[:, it:it + 1].unsqueeze(2),
                                o2b[:].unsqueeze(1), AX.X, ALU.add), inc=("v_red", 1))
                            c.op("vector", lambda e, o2b=o2b: e.tensor_tensor(
                                qsx[:, 0:D], o2b[:], o2b[:], ALU.mult))
                            c.op("vector", lambda e, it=it: e.tensor_reduce(
                                stats_v2[:, it:it + 1].unsqueeze(2),
                                qsx[:, 0:D].unsqueeze(1), AX.X, ALU.add), inc=("v_red", 1))
                            c.op("vector", _nop, inc=("s_sq2", 1))
                            c.wait("sync", "v_o2", c.val["v_o2"])
                            c.dma("sync", out=o2_s[b, ti * 128:(ti + 1) * 128, :],
                                  in_=o2b[:], sem="st_o2")
                c.wait("sync", "st_o2", c.val["st_o2"])
        c.barrier("b")

    def phase_c():
        with ExitStack() as ph:
            sb = lambda name, shape, dt=F32: ph.enter_context(nc.sbuf_tensor(name, shape, dt))
            o2c = [sb(f"o2c{i}", [128, D]) for i in range(2)]
            oc = [sb(f"oc{i}", [128, D]) for i in range(2)]
            emit_rstd(c, stats_m2, stats_v2, rstd2, "v_red", None)
            c.op("vector", _nop, inc=("v_m2", 1))
            c.wait("vector", "v_rstd_done", 2)
            for it in range(NTILES):
                b, ti = divmod(it, NT)
                ib, ub = o2c[it % 2], oc[it % 2]
                if it >= 2:
                    c.wait("sync", "v_out", it - 1)
                c.dma("sync", out=ib[:], in_=o2_s[b, ti * 128:(ti + 1) * 128, :],
                      sem=f"ld_oc{it % 2}")
                c.wait("vector", f"ld_oc{it % 2}", (it // 2 + 1) * 16)
                if it >= 2:
                    c.wait("vector", "st_out", (it - 1) * 16)
                c.op("vector", lambda e, ib=ib, ub=ub, it=it: e.tensor_scalar(
                    ub[:], ib[:], stats_m2[:, it:it + 1], rstd2[:, it:it + 1],
                    ALU.subtract, ALU.mult), inc=("v_out", 1))
                c.wait("sync", "v_out", it + 1)
                c.dma("sync", out=out_d[b, ti * 128:(ti + 1) * 128, :], in_=ub[:],
                      sem="st_out")
            c.wait("sync", "st_out", c.val["st_out"])

    phase_p0()
    if "a1" in phases:
        phase_a1()
    if "a2" in phases:
        phase_a2()
    if "a3" in phases:
        phase_a3()
    if "b" in phases:
        phase_b()
    if "c" in phases:
        phase_c()
    c.replay()
    est.close()
    return nc


_NC_CACHE = {}


def _get_nc():
    if "nc" not in _NC_CACHE:
        _NC_CACHE["nc"] = build_program()
    return _NC_CACHE["nc"]


def kernel(L0_attn_hidden_states, L0_attn_Wqkv_kernel, L0_attn_Wqkv_bias,
           L0_attn_Wo_kernel, L0_attn_Wo_bias, L0_attn_rope_cache,
           L0_ffn_w_gate, L0_ffn_w_up, L0_ffn_w_down):
    x = np.ascontiguousarray(np.asarray(L0_attn_hidden_states, dtype=np.float32))
    common = {
        "wqkv": np.ascontiguousarray(np.asarray(L0_attn_Wqkv_kernel, np.float32)),
        "bqkv": np.ascontiguousarray(np.asarray(L0_attn_Wqkv_bias, np.float32)),
        "wo": np.ascontiguousarray(np.asarray(L0_attn_Wo_kernel, np.float32)),
        "bo": np.ascontiguousarray(np.asarray(L0_attn_Wo_bias, np.float32)),
        "rope": np.ascontiguousarray(np.asarray(L0_attn_rope_cache, np.float32)),
        "wg": np.ascontiguousarray(np.asarray(L0_ffn_w_gate, np.float32)),
        "wu": np.ascontiguousarray(np.asarray(L0_ffn_w_up, np.float32)),
        "wd": np.ascontiguousarray(np.asarray(L0_ffn_w_down, np.float32)),
    }
    nc = _get_nc()
    in_maps = [dict(common, x=x[i * BB:(i + 1) * BB]) for i in range(NCORES)]
    res = run_bass_kernel_spmd(nc, in_maps, core_ids=list(range(NCORES)))
    return np.concatenate([r["out"] for r in res.results], axis=0)
